# version 30
# baseline (speedup 1.0000x reference)
import contextlib
import numpy as np

import concourse.bass as bass
import concourse.mybir as mybir
from concourse.bass_utils import run_bass_kernel_spmd
from concourse.alu_op_type import AluOpType

F32 = mybir.dt.float32
BF16 = mybir.dt.bfloat16
AF = mybir.ActivationFunctionType
AX = mybir.AxisListType
A = AluOpType

M32C = 12582912.0
M64C = 6755399441055744.0
EPS_LN = 1e-5
EPS_FFN = 1e-8
EPS_RMS = 1.1920928955078125e-07

ENG = ("sync", "tensor", "vector", "scalar", "gpsimd")


class Cfg:
    def __init__(self, T=1024, D=2048, H=8192, phases=("ffn1", "attn", "ffn2")):
        self.T, self.D, self.H = T, D, H
        self.BS = 256
        self.phases = phases
        assert T % self.BS == 0 and D % 512 == 0 and H % 128 == 0
        self.nT = T // 128
        self.nD = D // 128
        self.nH = H // 128
        self.nC = T // self.BS
        self.TG = min(512, T)
        self.nTG = T // self.TG
        self.SPAN = min(1024, H)
        self.nSPAN = H // self.SPAN
        self.spanH = self.SPAN // 128
        self.DOS = min(512, D)
        self.nDO = D // self.DOS
        self.OG = min(128, D)
        self.nOG = D // self.OG


FLUSH_N = 512


def _blk(nc, **fns):
    with nc.Block(no_gpsimd_drain=True) as blk:
        for name, fn in fns.items():
            getattr(blk, name)(fn)


class Ctx:
    def __init__(self, nc, es):
        self.nc = nc
        self.sem = {}
        self.cnt = {}
        self.mark = {e: {} for e in ENG}
        self.cur_blk = None
        self.blk_n = 0
        for name in ("qs", "qg", "te", "ve", "se", "gp", "cc", "qsp", "qst",
                     "qxc", "qxd",
                     "qa", "qb", "qa1", "qb1", "qw", "qw0", "qw1", "qwd",
                     "qc0", "qc1", "qga", "qgb",
                     "qsa", "qsb", "qxa", "qxb", "qy0", "qy1"):
            self.sem[name] = es.enter_context(nc.semaphore(name))
            self.cnt[name] = 0

    def emit(self, **fns):
        if self.cur_blk is None:
            self.cur_blk = self.nc.Block(no_gpsimd_drain=True)
            self.cur_blk.__enter__()
            self.blk_n = 0
        for name, fn in fns.items():
            getattr(self.cur_blk, name)(fn)
        self.blk_n += 1
        if self.blk_n >= FLUSH_N:
            self.flush()

    def flush(self):
        if self.cur_blk is not None:
            self.cur_blk.__exit__(None, None, None)
            self.cur_blk = None

    def inc(self, instr, sname, amt=1):
        instr.then_inc(self.sem[sname], amt)
        self.cnt[sname] += amt
        return self.cnt[sname]

    def wait(self, e, ename, sname, val=None):
        if val is None:
            val = self.cnt[sname]
        if val <= 0:
            return
        wm = self.mark[ename]
        if wm.get(sname, 0) >= val:
            return
        e.wait_ge(self.sem[sname], val)
        wm[sname] = val

    def tok(self):
        return dict(self.cnt)

    def wait_tok(self, e, ename, token):
        for sname, val in token.items():
            self.wait(e, ename, sname, val)

    def barrier(self, nc):
        self.flush()
        t = self.tok()
        t.pop("cc", None)
        fns = {}
        for ename in ENG:
            fns[ename] = (lambda en: lambda e: self.wait_tok(e, en, t))(ename)
        _blk(nc, **fns)


def build_nc(cfg: Cfg):
    nc = bass.Bass("TRN2", target_bir_lowering=False)
    T, D, H = cfg.T, cfg.D, cfg.H
    nT, nD, nC = cfg.nT, cfg.nD, cfg.nC

    x0_t = nc.dram_tensor("x0", [T, D], F32, kind="ExternalInput")
    OGp, nOGp = cfg.OG, cfg.nOG
    wqT_t = nc.dram_tensor("wqT", [nOGp * 128, cfg.nD * OGp], F32, kind="ExternalInput")
    wkT_t = nc.dram_tensor("wkT", [nOGp * 128, cfg.nD * OGp], F32, kind="ExternalInput")
    wvT_t = nc.dram_tensor("wvT", [nOGp * 128, cfg.nD * OGp], F32, kind="ExternalInput")
    wgT_t = nc.dram_tensor("wgT", [nOGp * 128, cfg.nD * OGp], F32, kind="ExternalInput")
    wpT_t = nc.dram_tensor("wpT", [cfg.nDO * 128, cfg.nD * cfg.DOS], F32, kind="ExternalInput")
    fgT_t = nc.dram_tensor("fgT", [cfg.nH * 128, cfg.nD * 128], F32, kind="ExternalInput")
    fuT_t = nc.dram_tensor("fuT", [cfg.nH * 128, cfg.nD * 128], F32, kind="ExternalInput")
    fdT_t = nc.dram_tensor("fdT", [H, D], F32, kind="ExternalInput")
    pw_t = nc.dram_tensor("prefw", [1, 4], F32, kind="ExternalInput")
    out_t = nc.dram_tensor("out", [T, D], F32, kind="ExternalOutput")

    aqC_t = nc.dram_tensor("aqC", [nOGp * 128, cfg.nD * OGp], BF16)
    akC_t = nc.dram_tensor("akC", [nOGp * 128, cfg.nD * OGp], BF16)
    avC_t = nc.dram_tensor("avC", [nOGp * 128, cfg.nD * OGp], BF16)
    agC_t = nc.dram_tensor("agC", [nOGp * 128, cfg.nD * OGp], BF16)
    apC_t = nc.dram_tensor("apC", [cfg.nDO * 128, cfg.nD * cfg.DOS], BF16)
    sgC_t = nc.dram_tensor("sgC", [cfg.nH * 128, cfg.nD * 128], BF16)
    suC_t = nc.dram_tensor("suC", [cfg.nH * 128, cfg.nD * 128], BF16)
    sdC_t = nc.dram_tensor("sdC", [H, D], BF16)
    x1_t = nc.dram_tensor("x1", [T, D], F32)
    x2_t = nc.dram_tensor("x2", [T, D], F32)
    xnT_t = nc.dram_tensor("xnT_spill", [D, T], BF16)
    HD = cfg.nD * D // 2
    cc_in_h = [nc.dram_tensor(f"cc_in{i}", [128, HD], BF16) for i in range(2)]
    cc_out_h = [nc.dram_tensor(f"cc_out{i}", [4 * 128, HD], BF16) for i in range(2)]

    es = contextlib.ExitStack()
    with es:
        ctx = Ctx(nc, es)

        _uniq = [0]

        def sb(stk, name, shape, dt, side=None):
            ctx.flush()
            _uniq[0] += 1
            if side is None:
                return stk.enter_context(nc.sbuf_tensor(f"{name}_{_uniq[0]}", shape, dt))
            return stk.enter_context(nc.sbuf_tensor(f"{name}_{_uniq[0]}", shape, dt,
                                                    side=side))

        def psb(stk, name, shape, dt):
            ctx.flush()
            _uniq[0] += 1
            return stk.enter_context(nc.psum_tensor(f"{name}_{_uniq[0]}", shape, dt))

        cast_pieces = []
        cast_tok = {}

        def emit_cast_piece(n=1):
            for _ in range(n):
                if not cast_pieces:
                    return
                kind, srcw, dstw, r0, rn = cast_pieces.pop(0)

                def f_cp(e, kind=kind, srcw=srcw, dstw=dstw, r0=r0, rn=rn):
                    ctx.inc(e.dma_start(out=dstw.ap()[r0:r0 + rn, :],
                                        in_=srcw.ap()[r0:r0 + rn, :]), "qg", 16)
                    cast_tok[kind] = ctx.cnt["qg"]

                ctx.emit(gpsimd=f_cp)

        ident32 = sb(es, "ident32", [128, 128], F32)
        identb = sb(es, "identb", [128, 128], BF16)
        maskT = [sb(es, "maskTa", [128, 256], BF16), sb(es, "maskTb", [128, 256], BF16)]
        magic = sb(es, "magic", [128, 1], F32)
        prefw = sb(es, "prefw_sb", [128, 4], F32)
        sAp = [sb(es, f"sA{i}", [128, 1], F32) for i in range(4)]
        sBp = [sb(es, f"sB{i}", [128, 1], F32) for i in range(4)]
        sCp = [sb(es, f"sC{i}", [128, 1], F32) for i in range(4)]
        sA, sB, sC = sAp[0], sBp[0], sCp[0]

        def emit_consts():
            with contextlib.ExitStack() as stk:
                io_r = sb(stk, "io_r", [128, 256], F32)
                io_c = sb(stk, "io_c", [128, 256], F32)

                def f_sync(e):
                    pass

                def f_gp(e):
                    ctx.inc(e.iota(io_r[:], [[0, 256]], channel_multiplier=1,
                                   allow_small_or_imprecise_dtypes=True), "gp")
                    ctx.inc(e.iota(io_c[:], [[1, 256]], channel_multiplier=0,
                                   allow_small_or_imprecise_dtypes=True), "gp")
                    ctx.inc(e.dma_start(out=prefw[:],
                                        in_=bass.AP(pw_t, 0, [[0, 128], [1, 4]])), "qg", 16)

                def f_ve(e):
                    ctx.wait(e, "vector", "gp")
                    ctx.inc(e.tensor_tensor(out=ident32[:], in0=io_r[:, 0:128],
                                            in1=io_c[:, 0:128], op=A.is_equal), "ve")
                    ctx.inc(e.tensor_tensor(out=identb[:], in0=io_r[:, 0:128],
                                            in1=io_c[:, 0:128], op=A.is_equal), "ve")
                    ctx.inc(e.tensor_tensor(out=maskT[0][:], in0=io_c[:], in1=io_r[:],
                                            op=A.is_ge), "ve")
                    ctx.wait(e, "vector", "ve")
                    ctx.inc(e.tensor_scalar(out=io_r[:], in0=io_r[:], scalar1=128.0,
                                            scalar2=None, op0=A.add), "ve")
                    ctx.wait(e, "vector", "ve")
                    ctx.inc(e.tensor_tensor(out=maskT[1][:], in0=io_c[:], in1=io_r[:],
                                            op=A.is_ge), "ve")
                    ctx.inc(e.memset(sA[:], 1e-9), "ve")
                    ctx.wait(e, "vector", "ve")
                    ctx.inc(e.tensor_scalar(out=sA[:], in0=sA[:], scalar1=1.0, scalar2=1.0,
                                            op0=A.add, op1=A.is_gt), "ve")
                    ctx.wait(e, "vector", "ve")
                    ctx.inc(e.tensor_scalar(out=magic[:], in0=sA[:], scalar1=M64C - M32C,
                                            scalar2=M32C, op0=A.mult, op1=A.add), "ve")

                ctx.emit(sync=f_sync, gpsimd=f_gp, vector=f_ve)
                ctx.barrier(nc)

        def emit_stage_a(src_t, dstT, psTfp, zscAp, zscBp, xtile_fn, attn_mode, spill):
            ve_done = {}
            se_done = {}
            psTf_evac = [0, 0]
            st = {}
            NPAR = len(zscAp)

            def tile_steps(tt, se_ops):
                par = tt % NPAR
                sA, sB, sC = sAp[par], sBp[par], sCp[par]
                zscA, zscB = zscAp[par], zscBp[par]
                xt = xtile_fn(tt)
                qld = ("qxa", "qxb", "qxc", "qxd")[par]
                steps = []

                def s_load():
                    def f_sync(e):
                        ctx.wait(e, "sync", "ve", ve_done.get(tt - NPAR, 0))
                        ctx.wait(e, "sync", "se", se_done.get(tt - NPAR, 0))
                        ctx.inc(e.dma_start(out=xt,
                                            in_=src_t.ap()[tt * 128:(tt + 1) * 128, :]),
                                qld, 16)
                    ctx.emit(sync=f_sync)
                    st[(tt, "ld")] = ctx.cnt[qld]
                steps.append(s_load)

                def sq1_op(e):
                    ctx.wait(e, "scalar", qld, st[(tt, "ld")])
                    ctx.wait(e, "scalar", "ve", st.get((tt - NPAR, "zA_done"), 0))
                    n = ctx.inc(e.activation(out=zscA[:], in_=xt, func=AF.Square,
                                             accum_out=sB[:]), "se")
                    assert n == st[(tt, "sq")]
                se_ops[0].append(sq1_op)

                def s_stats1():
                    def f_ve(e):
                        ctx.wait(e, "vector", qld, st[(tt, "ld")])
                        ctx.inc(e.reduce_sum(out=sA[:], in_=xt, axis=AX.X), "ve")
                    ctx.emit(vector=f_ve)
                    st[(tt, "r1")] = ctx.cnt["ve"]
                steps.append(s_stats1)

                def s_comb1():
                    def f_ve(e):
                        ctx.wait(e, "vector", "ve", st[(tt, "r1")])
                        ctx.wait(e, "vector", "se", st[(tt, "sq")])
                        ctx.inc(e.tensor_tensor(out=sC[:], in0=sA[:], in1=sA[:],
                                                op=A.mult), "ve")
                        ctx.wait(e, "vector", "ve")
                        ctx.inc(e.scalar_tensor_tensor(out=sC[:], in0=sC[:],
                                                       scalar=-1.0 / D, in1=sB[:],
                                                       op0=A.mult, op1=A.add), "ve")
                    ctx.emit(vector=f_ve)
                    st[(tt, "w")] = ctx.cnt["ve"]
                steps.append(s_comb1)

                def sqrt1_op(e):
                    ctx.wait(e, "scalar", "ve", st[(tt, "wv")])
                    n = ctx.inc(e.activation(out=sC[:], in_=sC[:], func=AF.Sqrt), "se")
                    assert n == st[(tt, "std")]
                se_ops[1].append(sqrt1_op)

                def s_sqrt1():
                    def f_ve(e):
                        ctx.wait(e, "vector", "ve", st[(tt, "w")])
                        ctx.inc(e.tensor_scalar(out=sC[:], in0=sC[:], scalar1=1.0 / D,
                                                scalar2=EPS_LN, op0=A.mult, op1=A.add),
                                "ve")
                    ctx.emit(vector=f_ve)
                    st[(tt, "wv")] = ctx.cnt["ve"]
                steps.append(s_sqrt1)

                def s_apply1():
                    def f_ve(e):
                        ctx.wait(e, "vector", "se", st[(tt, "std")])
                        ctx.inc(e.reciprocal(out=sC[:], in_=sC[:]), "ve")
                        ctx.wait(e, "vector", "ve")
                        ctx.inc(e.scalar_tensor_tensor(out=sA[:], in0=sA[:],
                                                       scalar=1.0 / D, in1=sC[:],
                                                       op0=A.mult, op1=A.mult), "ve")
                        ctx.wait(e, "vector", "ve")
                        ctx.inc(e.tensor_scalar(out=zscA[:], in0=xt, scalar1=sC[:, 0:1],
                                                scalar2=sA[:, 0:1], op0=A.mult,
                                                op1=A.subtract), "ve")
                    ctx.emit(vector=f_ve)
                    st[(tt, "y")] = ctx.cnt["ve"]
                steps.append(s_apply1)

                if not attn_mode:
                    def sq2_op(e):
                        ctx.wait(e, "scalar", "ve", st[(tt, "y")])
                        n = ctx.inc(e.activation(out=zscB[:], in_=zscA[:],
                                                 func=AF.Square, accum_out=sB[:]), "se")
                        assert n == st[(tt, "sq2")]
                    se_ops[2].append(sq2_op)

                    def s_stats2():
                        def f_ve(e):
                            ctx.wait(e, "vector", "ve", st[(tt, "y")])
                            ctx.inc(e.reduce_sum(out=sA[:], in_=zscA[:], axis=AX.X), "ve")
                        ctx.emit(vector=f_ve)
                        st[(tt, "r2")] = ctx.cnt["ve"]
                    steps.append(s_stats2)

                    def s_comb2():
                        def f_ve(e):
                            ctx.wait(e, "vector", "ve", st[(tt, "r2")])
                            ctx.wait(e, "vector", "se", st[(tt, "sq2")])
                            ctx.inc(e.tensor_tensor(out=sC[:], in0=sA[:], in1=sA[:],
                                                    op=A.mult), "ve")
                            ctx.wait(e, "vector", "ve")
                            ctx.inc(e.scalar_tensor_tensor(out=sC[:], in0=sC[:],
                                                           scalar=-1.0 / D, in1=sB[:],
                                                           op0=A.mult, op1=A.add), "ve")
                        ctx.emit(vector=f_ve)
                        st[(tt, "w2")] = ctx.cnt["ve"]
                    steps.append(s_comb2)

                    def sqrt2_op(e):
                        ctx.wait(e, "scalar", "ve", st[(tt, "w2v")])
                        n = ctx.inc(e.activation(out=sC[:], in_=sC[:], func=AF.Sqrt),
                                    "se")
                        assert n == st[(tt, "std2")]
                    se_ops[3].append(sqrt2_op)

                    def s_sqrt2():
                        def f_ve(e):
                            ctx.wait(e, "vector", "ve", st[(tt, "w2")])
                            ctx.inc(e.tensor_scalar(out=sC[:], in0=sC[:], scalar1=1.0 / D,
                                                    scalar2=EPS_FFN, op0=A.mult,
                                                    op1=A.add), "ve")
                        ctx.emit(vector=f_ve)
                        st[(tt, "w2v")] = ctx.cnt["ve"]
                    steps.append(s_sqrt2)

                    def s_apply2():
                        def f_ve(e):
                            ctx.wait(e, "vector", "se", st[(tt, "std2")])
                            ctx.inc(e.reciprocal(out=sC[:], in_=sC[:]), "ve")
                            ctx.wait(e, "vector", "ve")
                            ctx.inc(e.scalar_tensor_tensor(out=sA[:], in0=sA[:],
                                                           scalar=1.0 / D, in1=sC[:],
                                                           op0=A.mult, op1=A.mult), "ve")
                            ctx.wait(e, "vector", "ve")
                            ctx.inc(e.tensor_scalar(out=zscB[:], in0=zscA[:],
                                                    scalar1=sC[:, 0:1], scalar2=sA[:, 0:1],
                                                    op0=A.mult, op1=A.subtract), "ve")
                        ctx.emit(vector=f_ve)
                        st[(tt, "z")] = ctx.cnt["ve"]
                    steps.append(s_apply2)

                    def s_amax():
                        def f_ve(e):
                            ctx.wait(e, "vector", "ve", st[(tt, "z")])
                            ctx.inc(e.reduce_max(out=sA[:], in_=zscB[:], axis=AX.X,
                                                 apply_absolute_value=True), "ve")
                            ctx.wait(e, "vector", "ve")
                            ctx.inc(e.reciprocal(out=sB[:], in_=sA[:]), "ve")
                            ctx.wait(e, "vector", "ve")
                            ctx.inc(e.tensor_scalar(out=sB[:], in0=sB[:], scalar1=127.0,
                                                    scalar2=None, op0=A.mult), "ve")
                            ctx.wait(e, "vector", "ve")
                            ctx.inc(e.reciprocal(out=sC[:], in_=sB[:]), "ve")
                        ctx.emit(vector=f_ve)
                        st[(tt, "sinv")] = ctx.cnt["ve"]
                    steps.append(s_amax)

                    def s_round():
                        def f_ve(e):
                            ctx.wait(e, "vector", "ve", st[(tt, "sinv")])
                            ctx.inc(e.tensor_scalar(out=zscA[:], in0=zscB[:],
                                                    scalar1=sB[:, 0:1],
                                                    scalar2=magic[:, 0:1], op0=A.mult,
                                                    op1=A.add), "ve")
                            ctx.wait(e, "vector", "ve")
                            ctx.inc(e.tensor_scalar(out=zscB[:], in0=zscA[:],
                                                    scalar1=magic[:, 0:1],
                                                    scalar2=sC[:, 0:1], op0=A.subtract,
                                                    op1=A.mult), "ve")
                        ctx.emit(vector=f_ve)
                        st[(tt, "zB")] = ctx.cnt["ve"]
                        st[(tt, "zA_done")] = ctx.cnt["ve"]
                    steps.append(s_round)
                else:
                    def sq2_op(e):
                        ctx.wait(e, "scalar", "ve", st[(tt, "y")])
                        n = ctx.inc(e.activation(out=zscB[:], in_=zscA[:],
                                                 func=AF.Square, accum_out=sB[:]), "se")
                        assert n == st[(tt, "sq2")]
                    se_ops[2].append(sq2_op)

                    def sqrt2_op(e):
                        ctx.wait(e, "scalar", "ve", st[(tt, "w2v")])
                        n = ctx.inc(e.activation(out=sB[:], in_=sB[:], func=AF.Sqrt),
                                    "se")
                        assert n == st[(tt, "std2")]
                    se_ops[3].append(sqrt2_op)

                    def s_rms():
                        def f_ve2(e):
                            ctx.wait(e, "vector", "se", st[(tt, "sq2")])
                            ctx.inc(e.tensor_scalar(out=sB[:], in0=sB[:], scalar1=1.0 / D,
                                                    scalar2=EPS_RMS, op0=A.mult,
                                                    op1=A.add), "ve")
                        ctx.emit(vector=f_ve2)
                        st[(tt, "w2v")] = ctx.cnt["ve"]
                    steps.append(s_rms)

                    def s_rmsapply():
                        def f_ve(e):
                            ctx.wait(e, "vector", "se", st[(tt, "std2")])
                            ctx.inc(e.reciprocal(out=sB[:], in_=sB[:]), "ve")
                            ctx.wait(e, "vector", "ve")
                            ctx.inc(e.tensor_scalar(out=zscB[:], in0=zscA[:],
                                                    scalar1=sB[:, 0:1], scalar2=None,
                                                    op0=A.mult), "ve")
                        ctx.emit(vector=f_ve)
                        st[(tt, "zB")] = ctx.cnt["ve"]
                        st[(tt, "zA_done")] = ctx.cnt["ve"]
                    steps.append(s_rmsapply)

                def s_transpose():
                    for ri, r0 in enumerate(range(0, nD, 4)):
                        rn = min(4, nD - r0)
                        pfi = (tt * ((nD + 3) // 4) + ri) % 2
                        pf = psTfp[pfi]

                        def f_te(e, r0=r0, rn=rn, pf=pf, pfi=pfi):
                            ctx.wait(e, "tensor", "ve", st[(tt, "zB")])
                            ctx.wait(e, "tensor", "ve", psTf_evac[pfi])
                            for k in range(rn):
                                ctx.inc(e.transpose(
                                    pf[:, k * 128:(k + 1) * 128],
                                    zscB[:, (r0 + k) * 128:(r0 + k + 1) * 128],
                                    ident32[:]), "te")

                        ctx.emit(tensor=f_te)
                        trtok = ctx.cnt["te"]

                        def f_ve9(e, r0=r0, rn=rn, pf=pf, pfi=pfi, trtok=trtok):
                            ctx.wait(e, "vector", "te", trtok)
                            ctx.inc(e.tensor_copy(
                                out=bass.AP(dstT, r0 * T + tt * 128,
                                            [[nD * T, 128], [T, rn], [1, 128]]),
                                in_=pf[:, 0:rn * 128]), "ve")
                            psTf_evac[pfi] = ctx.cnt["ve"]

                        ctx.emit(vector=f_ve9)

                    if spill:
                        def f_sp(e):
                            ctx.wait(e, "sync", "ve")
                            ctx.inc(e.dma_start(
                                out=bass.AP(xnT_t, tt * 128,
                                            [[T, 128], [128 * T, nD], [1, 128]]),
                                in_=bass.AP(dstT, tt * 128,
                                            [[nD * T, 128], [T, nD], [1, 128]])),
                                "qsp", 16)
                        ctx.emit(sync=f_sp)
                    ve_done[tt] = ctx.cnt["ve"]
                    se_done[tt] = ctx.cnt["se"]
                steps.append(s_transpose)
                return steps

            for pair in range(0, nT, NPAR):
                tiles = [tt for tt in range(pair, min(pair + NPAR, nT))]
                k = len(tiles)
                se0 = ctx.cnt["se"]
                for i, tt in enumerate(tiles):
                    st[(tt, "sq")] = se0 + 0 * k + i + 1
                    st[(tt, "std")] = se0 + 1 * k + i + 1
                    st[(tt, "sq2")] = se0 + 2 * k + i + 1
                    st[(tt, "std2")] = se0 + 3 * k + i + 1
                se_ops = [[] for _ in range(4)]
                stepl = [tile_steps(tt, se_ops) for tt in tiles]
                for i in range(max(len(s) for s in stepl)):
                    for s in stepl:
                        if i < len(s):
                            s[i]()

                def f_sem(e, ops=se_ops):
                    for row in ops:
                        for fn in row:
                            fn(e)

                ctx.emit(scalar=f_sem)
                assert ctx.cnt["se"] == se0 + 4 * k
                for i, tt in enumerate(tiles):
                    se_done[tt] = se0 + i + 1

        def emit_ffn(src_t, dst_t, cache=None):
            fs = contextlib.ExitStack()
            with fs:
                acc = sb(fs, "acc", [128, nT * D], F32)
                xqT = sb(fs, "xqT", [128, nD * T], BF16)

                spanH, SPAN, nSPAN = cfg.spanH, cfg.SPAN, cfg.nSPAN
                TG, nTG = cfg.TG, cfg.nTG
                DOS, nDO = cfg.DOS, cfg.nDO
                sk1 = fs.enter_context(contextlib.ExitStack())
                wgb = [sb(sk1, f"wgb{i}", [128, nD * 128], BF16) for i in range(2)]
                wub = [sb(sk1, f"wub{i}", [128, nD * 128], BF16) for i in range(2)]
                if cache != "read":
                    wst = [sb(sk1, f"wst{i}", [128, nD * 128], F32) for i in range(2)]
                    wstu = [sb(sk1, f"wstu{i}", [128, nD * 128], F32) for i in range(2)]

                if True:

                    gu_sign = {}
                    ustg_free = {}
                    gu_mm = {}
                    wd_sign = {}
                    wd_mm = {}
                    pT_mult = {}
                    psgu_evac2 = [[0, 0], [0, 0]]
                    stg_free = {}
                    psgu_evac = [0, 0]
                    psA_evac = [0, 0]

                    def fgT_ap(ht):
                        return bass.AP(fgT_t, ht * 128 * nD * 128, [[nD * 128, 128], [1, nD * 128]])

                    def fuT_ap(ht):
                        return bass.AP(fuT_t, ht * 128 * nD * 128, [[nD * 128, 128], [1, nD * 128]])

                    def load_gu(ht, sgn=None, post=None):
                        b = ht % 2
                        qag = "qa" if b == 0 else "qa1"
                        qbg = "qb" if b == 0 else "qb1"

                        if cache == "read":
                            def f_sync(e, ht=ht, b=b, qag=qag, qbg=qbg):
                                if ht >= 2:
                                    ctx.wait(e, "sync", "te", gu_mm[ht - 2])
                                ctx.inc(e.dma_start(
                                    out=wgb[b][:],
                                    in_=sgC_t.ap()[ht * 128:(ht + 1) * 128, :]), qag, 16)
                                ctx.inc(e.dma_start(
                                    out=wub[b][:],
                                    in_=suC_t.ap()[ht * 128:(ht + 1) * 128, :]), qbg, 16)
                                gu_sign[ht] = ((qag, ctx.cnt[qag]), (qbg, ctx.cnt[qbg]))

                            ctx.emit(sync=f_sync)
                            return

                        def f_sync(e, ht=ht, b=b, qag=qag):
                            if ht >= 2:
                                ctx.wait(e, "sync", "se", stg_free[ht - 2])
                                ctx.wait(e, "sync", "te", gu_mm[ht - 2])
                            ctx.inc(e.dma_start(out=wst[b][:], in_=fgT_ap(ht)), qag, 16)

                        def f_se(e, ht=ht, b=b, qag=qag):
                            ctx.wait(e, "scalar", qag)
                            if cache == "write":
                                ctx.wait(e, "scalar", f"qw{b}")
                            ctx.inc(e.activation(out=wgb[b][:], in_=wst[b][:], func=AF.Sign),
                                    "se")
                            stg_free[ht] = ctx.cnt["se"]

                        def f_sync2(e, ht=ht, b=b, qbg=qbg):
                            if ht >= 2:
                                ctx.wait(e, "sync", "se", ustg_free[ht - 2])
                            ctx.inc(e.dma_start(out=wstu[b][:], in_=fuT_ap(ht)), qbg, 16)

                        def f_se2(e, ht=ht, b=b, qbg=qbg):
                            ctx.wait(e, "scalar", qbg)
                            if cache == "write":
                                ctx.wait(e, "scalar", f"qw{b}")
                            ctx.inc(e.activation(out=wub[b][:], in_=wstu[b][:], func=AF.Sign),
                                    "se")
                            gu_sign[ht] = (("se", ctx.cnt["se"]),)
                            stg_free[ht] = ctx.cnt["se"]
                            ustg_free[ht] = ctx.cnt["se"]

                        def emit_wr():
                            gtok = ctx.cnt["se"]

                            def f_wr(e, ht=ht, b=b, gtok=gtok):
                                ctx.wait(e, "gpsimd", "se", gtok)
                                ctx.inc(e.dma_start(
                                    out=sgC_t.ap()[ht * 128:(ht + 1) * 128, :],
                                    in_=wgb[b][:]), f"qw{b}", 16)
                                ctx.inc(e.dma_start(
                                    out=suC_t.ap()[ht * 128:(ht + 1) * 128, :],
                                    in_=wub[b][:]), f"qw{b}", 16)

                            ctx.emit(gpsimd=f_wr)

                        if sgn is None:
                            ctx.emit(sync=f_sync, scalar=f_se)
                            ctx.emit(sync=f_sync2, scalar=f_se2)
                            if cache == "write":
                                emit_wr()
                        else:
                            ctx.emit(sync=f_sync)
                            ctx.emit(sync=f_sync2)
                            sgn.extend([f_se, f_se2])
                            if cache == "write":
                                post.append(emit_wr)

                    def load_wd(sp):
                        if cache == "read":
                            def f_sync(e, sp=sp):
                                if sp >= 1:
                                    ctx.wait(e, "sync", "te", wd_mm[sp - 1])
                                ctx.inc(e.dma_start(
                                    out=wdb[:],
                                    in_=bass.AP(sdC_t, sp * SPAN * D,
                                                [[D, 128], [128 * D, spanH], [1, D]])),
                                    "qc0", 16)
                                wd_sign[sp] = (("qc0", ctx.cnt["qc0"]),)

                            ctx.emit(sync=f_sync)
                            return
                        raise AssertionError("write-mode load_wd replaced by load_wd_piece")

                    wd_pc_sign = {}

                    def load_wd_piece(sp, pc, sgn=None, post=None):
                        def f_sync(e, sp=sp, pc=pc):
                            if pc >= 1:
                                ctx.wait(e, "sync", "se", wd_pc_sign[pc - 1])
                            if pc == 0 and sp >= 1:
                                ctx.wait(e, "sync", "te", wd_mm[sp - 1])
                            ctx.inc(e.dma_start(
                                out=wdst[0][:],
                                in_=bass.AP(fdT_t, (sp * SPAN + pc * 128) * D,
                                            [[D, 128], [1, D]])), "qc0", 16)

                        def f_se(e, sp=sp, pc=pc):
                            ctx.wait(e, "scalar", "qc0")
                            if pc == 0 and sp >= 1:
                                ctx.wait(e, "scalar", "te", wd_mm[sp - 1])
                            if cache == "write":
                                ctx.wait(e, "scalar", "qwd")
                            ctx.inc(e.activation(out=wdb[:, pc * D:(pc + 1) * D],
                                                 in_=wdst[0][:], func=AF.Sign), "se")
                            wd_pc_sign[pc] = ctx.cnt["se"]
                            wd_sign[sp] = (("se", ctx.cnt["se"]),)

                        def emit_wr():
                            ptok = ctx.cnt["se"]

                            def f_wr(e, sp=sp, pc=pc, ptok=ptok):
                                ctx.wait(e, "gpsimd", "se", ptok)
                                ctx.inc(e.dma_start(
                                    out=bass.AP(sdC_t, (sp * SPAN + pc * 128) * D,
                                                [[D, 128], [1, D]]),
                                    in_=wdb[:, pc * D:(pc + 1) * D]), "qwd", 16)

                            ctx.emit(gpsimd=f_wr)

                        if sgn is None:
                            ctx.emit(sync=f_sync, scalar=f_se)
                            if cache == "write":
                                emit_wr()
                        else:
                            ctx.emit(sync=f_sync)
                            sgn.append(f_se)
                            if cache == "write":
                                post.append(emit_wr)

                    load_gu(0)
                    if cfg.nH > 1:
                        load_gu(1)

                    with contextlib.ExitStack() as sa:
                        nsa = 4 if cache == "read" else 2
                        zscAp = [sb(sa, f"zscA{i}", [128, D], F32) for i in range(nsa)]
                        zscBp = [sb(sa, f"zscB{i}", [128, D], F32) for i in range(nsa)]
                        psTfp = [psb(sa, f"psTf{i}", [128, 512], F32) for i in range(2)]
                        emit_stage_a(src_t, xqT, psTfp, zscAp, zscBp,
                                     lambda tt: acc[:, tt * D:(tt + 1) * D],
                                     attn_mode=False, spill=False)
                        ctx.barrier(nc)

                    sk = fs.enter_context(contextlib.ExitStack())
                    pT = sb(sk, "pT", [128, spanH * T], BF16)
                    wdb = sb(sk, "wdb", [128, spanH * D], BF16)
                    if cache != "read":
                        wdst = [sb(sk, "wdst0", [128, D], F32)]
                    gsc = sb(sk, "gsc", [128, T], BF16)
                    usc = sb(sk, "usc", [128, T], BF16)
                    psg = [psb(sk, f"psg{i}", [128, 512], F32) for i in range(nTG)]
                    psu = [psb(sk, f"psu{i}", [128, 512], F32) for i in range(nTG)]
                    DH = D // 2
                    DOSd = min(DOS, DH)
                    psA2 = [psb(sk, f"psA{i}", [128, DH], F32) for i in range(2)]

                    def down_proj(sp):
                        for tt in range(nT):
                            for half in range(2):
                                def f_te2(e, sp=sp, tt=tt, half=half):
                                    ctx.wait(e, "tensor", "ve",
                                             pT_mult[sp * spanH + spanH - 1])
                                    for sname, cval in wd_sign[sp]:
                                        ctx.wait(e, "tensor", sname, cval)
                                    ctx.wait(e, "tensor", "ve", psA_evac[half])
                                    ctx.wait(e, "tensor", "ve", psgu_evac2[1][0])
                                    ctx.wait(e, "tensor", "se", psgu_evac2[1][1])
                                    for hti in range(spanH):
                                        lhs = pT[:, hti * T + tt * 128:
                                                 hti * T + (tt + 1) * 128]
                                        for dsl in range(DH // DOSd):
                                            ds = half * (DH // DOSd) + dsl
                                            ctx.inc(e.matmul(
                                                psA2[half][:, dsl * DOSd:
                                                           (dsl + 1) * DOSd],
                                                lhs,
                                                wdb[:, hti * D + ds * DOSd:
                                                    hti * D + (ds + 1) * DOSd],
                                                start=(hti == 0),
                                                stop=(hti == spanH - 1)), "te")

                                ctx.emit(tensor=f_te2)
                                dn_tok = ctx.cnt["te"]

                                def f_ve4(e, tt=tt, half=half, dn_tok=dn_tok):
                                    ctx.wait(e, "vector", "te", dn_tok)
                                    ctx.inc(e.tensor_tensor(
                                        out=acc[:, tt * D + half * DH:
                                                tt * D + (half + 1) * DH],
                                        in0=psA2[half][:, 0:DH],
                                        in1=acc[:, tt * D + half * DH:
                                                tt * D + (half + 1) * DH],
                                        op=A.add), "ve")
                                    psA_evac[half] = ctx.cnt["ve"]

                                ctx.emit(vector=f_ve4)
                        wd_mm[sp] = ctx.cnt["te"]

                    for sp in range(nSPAN):
                        if cache == "read" and sp == 0:
                            load_wd(sp)
                        for hti in range(spanH):
                            ht = sp * spanH + hti
                            b = ht % 2

                            slot = hti % 2 if nTG == 2 else 0
                            if slot == 0:
                                pgl = [psg[tg][:, 0:TG] for tg in range(nTG)]
                                pul = [psu[tg][:, 0:TG] for tg in range(nTG)]
                            else:
                                pgl = [psA2[0][:, 0:TG], psA2[0][:, TG:2 * TG]]
                                pul = [psA2[1][:, 0:TG], psA2[1][:, TG:2 * TG]]

                            def f_te(e, ht=ht, b=b, slot=slot, pgl=pgl, pul=pul):
                                for sname, cval in gu_sign[ht]:
                                    ctx.wait(e, "tensor", sname, cval)
                                ctx.wait(e, "tensor", "ve", psgu_evac2[slot][0])
                                ctx.wait(e, "tensor", "se", psgu_evac2[slot][1])
                                if slot == 1:
                                    ctx.wait(e, "tensor", "ve", psA_evac[0])
                                    ctx.wait(e, "tensor", "ve", psA_evac[1])
                                for dc in range(nD):
                                    lg = wgb[b][:, dc * 128:(dc + 1) * 128]
                                    for tg in range(nTG):
                                        ctx.inc(e.matmul(
                                            pgl[tg], lg,
                                            xqT[:, dc * T + tg * TG: dc * T + (tg + 1) * TG],
                                            start=(dc == 0), stop=(dc == nD - 1)), "te")
                                    lu = wub[b][:, dc * 128:(dc + 1) * 128]
                                    for tg in range(nTG):
                                        ctx.inc(e.matmul(
                                            pul[tg], lu,
                                            xqT[:, dc * T + tg * TG: dc * T + (tg + 1) * TG],
                                            start=(dc == 0), stop=(dc == nD - 1)), "te")
                                gu_mm[ht] = ctx.cnt["te"]

                            ctx.emit(tensor=f_te)

                            if hti == 0 and sp >= 1:
                                down_proj(sp - 1)
                                if cache == "read":
                                    load_wd(sp)

                            sgn = []
                            post = []
                            if ht + 2 < cfg.nH:
                                load_gu(ht + 2, sgn, post)
                            if cache != "read" and hti < spanH:
                                load_wd_piece(sp, hti, sgn, post)

                            def f_se3(e, ht=ht, hti=hti, slot=slot, pgl=pgl,
                                      sgn=tuple(sgn)):
                                ctx.wait(e, "scalar", "te", gu_mm[ht])
                                if ht > 0:
                                    ctx.wait(e, "scalar", "ve", pT_mult[ht - 1])
                                for tg in range(nTG):
                                    ctx.inc(e.activation(
                                        out=gsc[:, tg * TG:(tg + 1) * TG],
                                        in_=pgl[tg], func=AF.Silu), "se")
                                psgu_evac2[slot][1] = ctx.cnt["se"]
                                for fn in sgn:
                                    fn(e)

                            def f_ve3(e, ht=ht, hti=hti, slot=slot, pul=pul):
                                ctx.wait(e, "vector", "te", gu_mm[ht])
                                if ht > 0:
                                    ctx.wait(e, "vector", "ve", pT_mult[ht - 1])
                                for tg in range(nTG):
                                    ctx.inc(e.tensor_scalar(
                                        out=usc[:, tg * TG:(tg + 1) * TG],
                                        in0=pul[tg], scalar1=0.5, scalar2=None,
                                        op0=A.mult), "ve")
                                psgu_evac2[slot][0] = ctx.cnt["ve"]
                                ctx.wait(e, "vector", "se")
                                ctx.wait(e, "vector", "ve")
                                if sp > 0:
                                    ctx.wait(e, "vector", "te", wd_mm[sp - 1])
                                ctx.inc(e.tensor_tensor(out=pT[:, hti * T:(hti + 1) * T],
                                                        in0=gsc[:], in1=usc[:], op=A.mult),
                                        "ve")
                                pT_mult[ht] = ctx.cnt["ve"]

                            ctx.emit(scalar=f_se3, vector=f_ve3)
                            for fn in post:
                                fn()
                            emit_cast_piece()
                    down_proj(nSPAN - 1)

                    def f_out(e):
                        ctx.wait(e, "sync", "ve")
                        for tt in range(nT):
                            ctx.inc(e.dma_start(out=dst_t.ap()[tt * 128:(tt + 1) * 128, :],
                                                in_=acc[:, tt * D:(tt + 1) * D]), "qst", 16)

                    ctx.emit(sync=f_out)
                    ctx.barrier(nc)

        def emit_attn(src_t, dst_t):
            BS = cfg.BS
            OG, nOG = cfg.OG, cfg.nOG
            at = contextlib.ExitStack()
            with at:
                mid = at.enter_context(contextlib.ExitStack())
                kT = sb(mid, "kT", [128, nD * T], BF16, side="right")
                vtok = sb(mid, "vtok", [128, nT * D], BF16, side="right")

                wload_tok = {}
                wpr_guard = [0, 0]
                psP_evac = {i: [0, 0] for i in range(4)}

                def proj_fm(w_t, dest, act, kind, wbufs, psP):
                    xnbuf = wbufs["xn"]
                    nOI = OG // 128
                    gsz = max(1, 4 // (nOI * cfg.nTG))
                    for og0 in range(0, nOG, gsz):
                        evl = []
                        for og in range(og0, min(og0 + gsz, nOG)):
                            b = og % 2

                            def f_qg(e, og=og, b=b):
                                ctx.wait(e, "sync", "qg", cast_tok.get(kind, 0))
                                ctx.wait(e, "sync", "te", wpr_guard[b])
                                ctx.inc(e.dma_start(
                                    out=wbufs["w"][b][:],
                                    in_=bass.AP(w_t, og * 128 * nD * OG,
                                                [[nD * OG, 128], [1, nD * OG]])),
                                        "qga" if b == 0 else "qgb", 16)

                            ctx.emit(sync=f_qg)

                            for oi in range(nOI):
                                for tg in range(cfg.nTG):
                                    pb = ((og * nOI + oi) * cfg.nTG + tg) % 4

                                    def f_te(e, og=og, b=b, oi=oi, tg=tg, pb=pb):
                                        ctx.wait(e, "tensor",
                                                 "qga" if b == 0 else "qgb")
                                        ctx.wait(e, "tensor", "ve", psP_evac[pb][0])
                                        ctx.wait(e, "tensor", "se", psP_evac[pb][1])
                                        for dc in range(nD):
                                            ctx.inc(e.matmul(
                                                psP[pb][:, 0:cfg.TG],
                                                wbufs["w"][b][:, dc * OG + oi * 128:
                                                             dc * OG + (oi + 1) * 128],
                                                xnbuf[:, dc * T + tg * cfg.TG:
                                                      dc * T + (tg + 1) * cfg.TG],
                                                start=(dc == 0),
                                                stop=(dc == nD - 1)), "te")

                                    ctx.emit(tensor=f_te)
                                    evl.append((pb, og * nOI + oi, tg,
                                                ctx.cnt["te"]))
                            wload_tok[(kind, og)] = ctx.cnt["te"]
                            wpr_guard[b] = ctx.cnt["te"]

                        def f_se(e, evl=tuple(evl)):
                            for pb, ot, tg, mmtok in evl:
                                ctx.wait(e, "scalar", "te", mmtok)
                                ctx.inc(e.activation(
                                    out=dest[:, ot * T + tg * cfg.TG:
                                             ot * T + (tg + 1) * cfg.TG],
                                    in_=psP[pb][:, 0:cfg.TG], func=act), "se")
                                psP_evac[pb][1] = ctx.cnt["se"]

                        ctx.emit(scalar=f_se)

                def proj_tm(w_t, dest, kind, wbufs, psP):
                    xnbuf = wbufs["xn"]
                    for og in range(nOG):
                        b = og % 2

                        def f_qg(e, og=og, b=b):
                            ctx.wait(e, "sync", "qg", cast_tok.get(kind, 0))
                            ctx.wait(e, "sync", "te", wpr_guard[b])
                            ctx.inc(e.dma_start(
                                out=wbufs["w"][b][:],
                                in_=bass.AP(w_t, og * 128 * nD * OG,
                                            [[nD * OG, 128], [1, nD * OG]])),
                                    "qga" if b == 0 else "qgb", 16)

                        ctx.emit(sync=f_qg)

                        se0 = ctx.cnt["se"]
                        mmtoks = []
                        for tt in range(nT):
                            pb = tt % 4

                            def f_te(e, og=og, b=b, tt=tt, pb=pb, se0=se0):
                                ctx.wait(e, "tensor", "qga" if b == 0 else "qgb")
                                ctx.wait(e, "tensor", "ve", psP_evac[pb][0])
                                if tt >= 4:
                                    ctx.wait(e, "tensor", "se", se0 + tt - 3)
                                else:
                                    ctx.wait(e, "tensor", "se", psP_evac[pb][1])
                                for dc in range(nD):
                                    ctx.inc(e.matmul(
                                        psP[pb][:, 0:OG],
                                        xnbuf[:, dc * T + tt * 128: dc * T + (tt + 1) * 128],
                                        wbufs["w"][b][:, dc * OG:(dc + 1) * OG],
                                        start=(dc == 0), stop=(dc == nD - 1)), "te")

                            ctx.emit(tensor=f_te)
                            mmtoks.append(ctx.cnt["te"])

                        def f_se(e, og=og, mmtoks=tuple(mmtoks)):
                            for tt in range(nT):
                                ctx.wait(e, "scalar", "te", mmtoks[tt])
                                ctx.inc(e.activation(
                                    out=dest[:, tt * D + og * OG:
                                             tt * D + (og + 1) * OG],
                                    in_=psP[tt % 4][:, 0:OG], func=AF.Silu), "se")

                        ctx.emit(scalar=f_se)
                        assert ctx.cnt["se"] == se0 + nT
                        for pb in range(min(4, nT)):
                            psP_evac[pb][1] = se0 + max(0, nT - 4) + pb + 1
                        wload_tok[(kind, og)] = ctx.cnt["te"]
                        wpr_guard[b] = ctx.cnt["te"]

                with contextlib.ExitStack() as s0:
                    xnT = sb(s0, "xnT", [128, nD * T], BF16)
                    zscAp = [sb(s0, f"zscA{i}", [128, D], F32) for i in range(4)]
                    xtilep = [sb(s0, f"xtile{i}", [128, D], F32) for i in range(4)]
                    zscBp = xtilep
                    wpr = [sb(s0, f"wpr{i}", [128, nD * OG], BF16) for i in range(2)]
                    psTfp = [psb(s0, f"psTf{i}", [128, 512], F32) for i in range(2)]
                    psP = [psb(s0, f"psP{i}", [128, 512], F32) for i in range(4)]

                    emit_stage_a(src_t, xnT, psTfp, zscAp, zscBp,
                                 lambda tt: xtilep[tt % 4][:],
                                 attn_mode=True, spill=True)
                    wb = {"xn": xnT, "w": wpr}
                    proj_fm(akC_t, kT, AF.Silu, "k", wb, psP)
                    proj_tm(avC_t, vtok, "v", wb, psP)
                    ctx.barrier(nc)

                with contextlib.ExitStack() as s1:
                    kv = sb(s1, "kv", [128, nD * D], BF16)
                    kts = sb(s1, "kts", [128, 2 * nD * 128], BF16)
                    psTb = [psb(s1, f"psTb{i}", [128, 1024], BF16) for i in range(2)]
                    psS = [psb(s1, f"psS{i}", [128, 512], F32) for i in range(2)]

                    psTb_evac = [0, 0]
                    psS_evac = [0, 0]
                    kv_state = {}
                    for c in range(nC):
                        for ri, r0 in enumerate(range(0, 2 * nD, 8)):
                            rn = min(8, 2 * nD - r0)
                            tb = ri % 2

                            def f_te(e, c=c, r0=r0, rn=rn, tb=tb):
                                ctx.wait(e, "tensor", "ve", psTb_evac[tb])
                                for k in range(rn):
                                    idx = r0 + k
                                    jj, dc = idx % 2, idx // 2
                                    ctx.inc(e.transpose(
                                        psTb[tb][:, k * 128:(k + 1) * 128],
                                        kT[:, dc * T + c * BS + jj * 128:
                                           dc * T + c * BS + (jj + 1) * 128],
                                        identb[:]), "te")

                            ctx.emit(tensor=f_te)
                            trtok = ctx.cnt["te"]

                            def f_ve(e, c=c, r0=r0, rn=rn, tb=tb, trtok=trtok):
                                ctx.wait(e, "vector", "te", trtok)
                                for k in range(rn):
                                    idx = r0 + k
                                    jj, dc = idx % 2, idx // 2
                                    ctx.inc(e.tensor_copy(
                                        out=kts[:, jj * nD * 128 + dc * 128:
                                                jj * nD * 128 + (dc + 1) * 128],
                                        in_=psTb[tb][:, k * 128:(k + 1) * 128]), "ve")
                                psTb_evac[tb] = ctx.cnt["ve"]

                            ctx.emit(vector=f_ve)
                        kts_tok = ctx.cnt["ve"]

                        for dt in range(nD):
                            for es in range(D // 512):
                                pb = (dt * (D // 512) + es) % 2

                                def f_te(e, c=c, dt=dt, es=es, pb=pb, kts_tok=kts_tok):
                                    ctx.wait(e, "tensor", "ve", kts_tok)
                                    if c > 0:
                                        ctx.wait(e, "tensor", "ve",
                                                 kv_state.get((dt, es), 0))
                                    ctx.wait(e, "tensor", "ve", psS_evac[pb])
                                    for jj in range(2):
                                        ctx.inc(e.matmul(
                                            psS[pb][:, 0:512],
                                            kts[:, jj * nD * 128 + dt * 128:
                                                jj * nD * 128 + (dt + 1) * 128],
                                            vtok[:, (2 * c + jj) * D + es * 512:
                                                 (2 * c + jj) * D + (es + 1) * 512],
                                            start=(jj == 0), stop=(jj == 1)), "te")

                                ctx.emit(tensor=f_te)
                                mmtok = ctx.cnt["te"]

                                def f_ve(e, c=c, dt=dt, es=es, pb=pb, mmtok=mmtok):
                                    ctx.wait(e, "vector", "te", mmtok)
                                    dst = kv[:, dt * D + es * 512: dt * D + (es + 1) * 512]
                                    if c == 0:
                                        ctx.inc(e.tensor_copy(out=dst,
                                                              in_=psS[pb][:, 0:512]), "ve")
                                    else:
                                        ctx.inc(e.tensor_tensor(out=dst,
                                                                in0=psS[pb][:, 0:512],
                                                                in1=dst, op=A.add), "ve")
                                    kv_state[(dt, es)] = ctx.cnt["ve"]
                                    psS_evac[pb] = ctx.cnt["ve"]

                                ctx.emit(vector=f_ve)

                    def f_snd(e):
                        ctx.wait(e, "sync", "ve")
                        ctx.inc(e.dma_start(out=cc_in_h[0].ap()[:, :], in_=kv[:, 0:HD]),
                                "qs", 16)
                        ctx.inc(e.dma_start(out=cc_in_h[1].ap()[:, :], in_=kv[:, HD:]),
                                "qs", 16)

                    ctx.emit(sync=f_snd)
                    ctx.barrier(nc)

                big = at.enter_context(contextlib.ExitStack())
                oT = sb(big, "oT", [128, nD * T], BF16)
                big2 = at.enter_context(contextlib.ExitStack())
                gT = sb(big2, "gT", [128, nD * T], BF16)
                qT = sb(big2, "qT", [128, nD * T], BF16)
                with contextlib.ExitStack() as s2:
                    xnT2 = sb(s2, "xnT2", [128, nD * T], BF16)
                    wpr2 = [sb(s2, f"wpr2{i}", [128, nD * OG], BF16) for i in range(2)]
                    sTs4 = sb(s2, "sTs4", [128, 4 * 2 * BS], BF16)
                    psP2 = [psb(s2, f"psP2{i}", [128, 512], F32) for i in range(4)]
                    psS2 = [psb(s2, f"psS2{i}", [128, 512], F32) for i in range(2)]
                    psO2 = [psb(s2, f"psO2{i}", [128, 512], F32) for i in range(2)]

                    def f_cc(e):
                        ctx.wait(e, "gpsimd", "qs")
                        for hh in range(2):
                            ctx.inc(e.collective_compute(
                                "AllGather", A.bypass,
                                replica_groups=[[0, 1, 2, 3], [4, 5, 6, 7]],
                                ins=[cc_in_h[hh].ap().opt()],
                                outs=[cc_out_h[hh].ap().opt()]), "cc")

                    def f_ld(e):
                        ctx.inc(e.dma_start(
                            out=xnT2[:],
                            in_=bass.AP(xnT_t, 0, [[T, 128], [128 * T, nD], [1, T]])),
                            "qs", 16)

                    ctx.emit(gpsimd=f_cc, sync=f_ld)
                    xn2tok = ctx.cnt["qs"]

                    def wait_xn2(e):
                        ctx.wait(e, "tensor", "qs", xn2tok)
                    psP_evac = {i: [0, 0] for i in range(4)}
                    wb2 = {"xn": xnT2, "w": wpr2}
                    _orig_emit = ctx.emit

                    def emit_with_xn(**fns):
                        if "tensor" in fns:
                            f = fns["tensor"]

                            def wrapped(e, f=f):
                                wait_xn2(e)
                                f(e)
                            fns["tensor"] = wrapped
                        _orig_emit(**fns)

                    ctx.emit = emit_with_xn
                    proj_fm(aqC_t, qT, AF.Silu, "q", wb2, psP2)
                    proj_fm(agC_t, gT, AF.Sigmoid, "g", wb2, psP2)
                    ctx.emit = _orig_emit

                    psS2_evac = [0, 0]
                    psO2_evac = [0, 0]
                    qT_tok = ctx.cnt["se"]
                    for c in range(nC):
                        for cp in range(c + 1):
                            for jt in range(2):
                                pb = (cp * 2 + jt) % 2

                                def f_te(e, c=c, cp=cp, jt=jt, pb=pb):
                                    ctx.wait(e, "tensor", "se", qT_tok)
                                    ctx.wait(e, "tensor", "ve", psS2_evac[pb])
                                    for dc in range(nD):
                                        ctx.inc(e.matmul(
                                            psS2[pb][:, 0:BS],
                                            kT[:, dc * T + cp * BS + jt * 128:
                                               dc * T + cp * BS + (jt + 1) * 128],
                                            qT[:, dc * T + c * BS: dc * T + c * BS + BS],
                                            start=(dc == 0), stop=(dc == nD - 1)), "te")

                                ctx.emit(tensor=f_te)
                                mmtok = ctx.cnt["te"]

                                def f_ve(e, c=c, cp=cp, jt=jt, pb=pb, mmtok=mmtok):
                                    ctx.wait(e, "vector", "te", mmtok)
                                    dst = sTs4[:, cp * 2 * BS + jt * BS:
                                               cp * 2 * BS + (jt + 1) * BS]
                                    if cp == c:
                                        ctx.inc(e.tensor_tensor(
                                            out=dst, in0=psS2[pb][:, 0:BS],
                                            in1=maskT[jt][:, 0:BS], op=A.mult), "ve")
                                    else:
                                        ctx.inc(e.tensor_copy(
                                            out=dst, in_=psS2[pb][:, 0:BS]), "ve")
                                    psS2_evac[pb] = ctx.cnt["ve"]

                                ctx.emit(vector=f_ve)
                        sblk_tok = ctx.cnt["ve"]

                        for et in range(nD):
                            ob = et % 2

                            def f_te(e, c=c, et=et, ob=ob, sblk_tok=sblk_tok):
                                ctx.wait(e, "tensor", "ve", sblk_tok)
                                ctx.wait(e, "tensor", "ve", psO2_evac[ob])
                                nmm = 2 * (c + 1)
                                i = 0
                                for cp in range(c + 1):
                                    for jj in range(2):
                                        ctx.inc(e.matmul(
                                            psO2[ob][:, 0:BS],
                                            vtok[:, (2 * cp + jj) * D + et * 128:
                                                 (2 * cp + jj) * D + (et + 1) * 128],
                                            sTs4[:, cp * 2 * BS + jj * BS:
                                                 cp * 2 * BS + (jj + 1) * BS],
                                            start=(i == 0), stop=(i == nmm - 1)), "te")
                                        i += 1

                            ctx.emit(tensor=f_te)
                            mmtok = ctx.cnt["te"]

                            def f_ve(e, c=c, et=et, ob=ob, mmtok=mmtok):
                                ctx.wait(e, "vector", "te", mmtok)
                                ctx.inc(e.tensor_tensor(
                                    out=oT[:, et * T + c * BS: et * T + c * BS + BS],
                                    in0=psO2[ob][:, 0:BS],
                                    in1=gT[:, et * T + c * BS: et * T + c * BS + BS],
                                    op=A.mult), "ve")
                                psO2_evac[ob] = ctx.cnt["ve"]

                            ctx.emit(vector=f_ve)
                    ctx.barrier(nc)

                ctx.flush()
                mid.close()

                with contextlib.ExitStack() as s3:
                    S = sb(s3, "S", [128, nD * D], BF16)
                    stg = [sb(s3, f"stg{i}", [128, 2 * D], BF16) for i in range(2)]
                    tmpg = sb(s3, "tmpg", [128, cfg.TG], F32)
                    psP3 = [psb(s3, f"psP3{i}", [128, 512], F32) for i in range(2)]

                    ld_tok = {}
                    S_half_tok = [0, 0]
                    PW = 2 * D
                    nPC = max(1, HD // PW)
                    PW = HD // nPC
                    uid = 0
                    for hh in range(2):
                        for p in range(3):
                            for pc in range(nPC):
                                b = uid % 2

                                def f_ld(e, hh=hh, p=p, pc=pc, b=b, uid=uid):
                                    ctx.wait(e, "sync", "cc", hh + 1)
                                    ctx.wait(e, "sync", "ve", ld_tok.get(uid - 2, 0))
                                    ctx.inc(e.dma_start(
                                        out=stg[b][:, 0:PW],
                                        in_=cc_out_h[hh].ap()[p * 128:(p + 1) * 128,
                                                              pc * PW:(pc + 1) * PW]),
                                        "qsa" if b == 0 else "qsb", 16)

                                ctx.emit(sync=f_ld)
                                dtok = ctx.cnt["qsa" if b == 0 else "qsb"]

                                def f_ve(e, hh=hh, p=p, pc=pc, b=b, dtok=dtok, uid=uid):
                                    ctx.wait(e, "vector", "qsa" if b == 0 else "qsb", dtok)
                                    if p > 0:
                                        ctx.wait(e, "vector", "ve", ld_tok[uid - nPC])
                                    dst = S[:, hh * HD + pc * PW: hh * HD + (pc + 1) * PW]
                                    if p == 0:
                                        ctx.inc(e.tensor_scalar(
                                            out=dst, in0=stg[b][:, 0:PW],
                                            scalar1=prefw[:, 0:1],
                                            scalar2=None, op0=A.mult), "ve")
                                    else:
                                        ctx.inc(e.scalar_tensor_tensor(
                                            out=dst, in0=stg[b][:, 0:PW],
                                            scalar=prefw[:, p:p + 1],
                                            in1=dst, op0=A.mult, op1=A.add), "ve")
                                    ld_tok[uid] = ctx.cnt["ve"]

                                ctx.emit(vector=f_ve)
                                uid += 1
                        S_half_tok[hh] = ctx.cnt["ve"]
                    S_tok = ctx.cnt["ve"]

                    psP3_evac = [0, 0]
                    for hh in range(2):
                        dlo, dhi = hh * (nD // 2), (hh + 1) * (nD // 2)
                        for et in range(nD):
                            for tg in range(cfg.nTG):
                                pb = ((hh * nD + et) * cfg.nTG + tg) % 2

                                def f_te(e, hh=hh, et=et, tg=tg, pb=pb,
                                         dlo=dlo, dhi=dhi):
                                    ctx.wait(e, "tensor", "ve", psP3_evac[pb])
                                    ctx.wait(e, "tensor", "ve", S_half_tok[hh])
                                    for dc in range(dlo, dhi):
                                        ctx.inc(e.matmul(
                                            psP3[pb][:, 0:cfg.TG],
                                            S[:, dc * D + et * 128:
                                              dc * D + (et + 1) * 128],
                                            qT[:, dc * T + tg * cfg.TG:
                                               dc * T + (tg + 1) * cfg.TG],
                                            start=(dc == dlo),
                                            stop=(dc == dhi - 1)), "te")

                                ctx.emit(tensor=f_te)
                                mmtok = ctx.cnt["te"]

                                def f_ve(e, et=et, tg=tg, pb=pb, mmtok=mmtok):
                                    ctx.wait(e, "vector", "te", mmtok)
                                    ctx.wait(e, "vector", "ve")
                                    sl = slice(et * T + tg * cfg.TG,
                                               et * T + (tg + 1) * cfg.TG)
                                    ctx.inc(e.tensor_tensor(
                                        out=tmpg[:], in0=psP3[pb][:, 0:cfg.TG],
                                        in1=gT[:, sl], op=A.mult), "ve")
                                    ctx.wait(e, "vector", "ve")
                                    ctx.inc(e.tensor_tensor(
                                        out=oT[:, sl], in0=tmpg[:],
                                        in1=oT[:, sl], op=A.add), "ve")
                                    psP3_evac[pb] = ctx.cnt["ve"]

                                ctx.emit(vector=f_ve)
                    ctx.barrier(nc)

                ctx.flush()
                big2.close()

                with contextlib.ExitStack() as s4:
                    wpo = [sb(s4, f"wpo{i}", [128, nD * 512], BF16) for i in range(2)]
                    x1s = [sb(s4, f"x1s{i}", [128, 512], F32) for i in range(2)]
                    x2s = [sb(s4, f"x2s{i}", [128, 512], F32) for i in range(2)]
                    psQ = [psb(s4, f"psQ{i}", [128, 512], F32) for i in range(8)]

                    psQ_evac = {i: 0 for i in range(8)}
                    wpo_tok = {}
                    x2_done = {}
                    for ds in range(cfg.nDO):
                        b = ds % 2

                        def f_qg(e, ds=ds, b=b):
                            ctx.wait(e, "sync", "qg", cast_tok.get("p", 0))
                            if ds >= 2:
                                ctx.wait(e, "sync", "te", wpo_tok[ds - 2])
                            ctx.inc(e.dma_start(
                                out=wpo[b][:],
                                in_=bass.AP(apC_t, ds * 128 * nD * 512,
                                            [[nD * 512, 128], [1, nD * 512]])),
                                    "qga" if b == 0 else "qgb", 16)

                        ctx.emit(sync=f_qg)

                        for tt in range(nT):
                            def f_te(e, ds=ds, b=b, tt=tt):
                                ctx.wait(e, "tensor", "qga" if b == 0 else "qgb")
                                ctx.wait(e, "tensor", "ve", psQ_evac[tt])
                                for ec in range(nD):
                                    ctx.inc(e.matmul(
                                        psQ[tt][:, 0:512],
                                        oT[:, ec * T + tt * 128: ec * T + (tt + 1) * 128],
                                        wpo[b][:, ec * 512:(ec + 1) * 512],
                                        start=(ec == 0), stop=(ec == nD - 1)), "te")

                            ctx.emit(tensor=f_te)
                            mmtok = ctx.cnt["te"]

                            sbuf = (ds * nT + tt) % 2
                            qx = "qxa" if sbuf == 0 else "qxb"
                            qy = "qy0" if sbuf == 0 else "qy1"

                            def f_ld(e, ds=ds, tt=tt, sbuf=sbuf, qx=qx):
                                ctx.wait(e, "sync", "ve", x2_done.get((ds * nT + tt) - 2, 0))
                                ctx.inc(e.dma_start(
                                    out=x1s[sbuf][:],
                                    in_=src_t.ap()[tt * 128:(tt + 1) * 128,
                                                   ds * 512:(ds + 1) * 512]), qx, 16)

                            ctx.emit(sync=f_ld)
                            ldtok = ctx.cnt[qx]

                            def f_ve(e, ds=ds, tt=tt, sbuf=sbuf, mmtok=mmtok, ldtok=ldtok,
                                     qx=qx, qy=qy):
                                ctx.wait(e, "vector", "te", mmtok)
                                ctx.wait(e, "vector", qx, ldtok)
                                ctx.wait(e, "vector", qy, ctx.cnt[qy])
                                ctx.inc(e.tensor_tensor(out=x2s[sbuf][:],
                                                        in0=psQ[tt][:, 0:512],
                                                        in1=x1s[sbuf][:], op=A.add), "ve")
                                psQ_evac[tt] = ctx.cnt["ve"]
                                x2_done[ds * nT + tt] = ctx.cnt["ve"]

                            ctx.emit(vector=f_ve)
                            vtk = ctx.cnt["ve"]

                            def f_st(e, ds=ds, tt=tt, sbuf=sbuf, vtk=vtk, qy=qy):
                                ctx.wait(e, "sync", "ve", vtk)
                                ctx.inc(e.dma_start(
                                    out=dst_t.ap()[tt * 128:(tt + 1) * 128,
                                                   ds * 512:(ds + 1) * 512],
                                    in_=x2s[sbuf][:]), qy, 16)

                            ctx.emit(sync=f_st)
                        wpo_tok[ds] = ctx.cnt["te"]
                    ctx.barrier(nc)

        emit_consts()
        if "attn" in cfg.phases:
            CPC = 8
            for kind, srcw, dstw in (("k", wkT_t, akC_t), ("v", wvT_t, avC_t),
                                     ("q", wqT_t, aqC_t), ("g", wgT_t, agC_t),
                                     ("p", wpT_t, apC_t)):
                rows = dstw.shape[0]
                rpp = max(128, rows // CPC)
                for r0 in range(0, rows, rpp):
                    cast_pieces.append((kind, srcw, dstw, r0,
                                        min(rpp, rows - r0)))
        src = x0_t
        both_ffn = "ffn1" in cfg.phases and "ffn2" in cfg.phases
        if "ffn1" in cfg.phases:
            dst1 = x1_t if ("attn" in cfg.phases or "ffn2" in cfg.phases) else out_t
            emit_ffn(src, dst1, cache="write" if both_ffn else None)
            src = dst1
        if "attn" in cfg.phases:
            emit_cast_piece(len(cast_pieces))
            dst2 = x2_t if "ffn2" in cfg.phases else out_t
            emit_attn(src, dst2)
            src = dst2
        if "ffn2" in cfg.phases:
            emit_ffn(src, out_t, cache="read" if both_ffn else None)

        ctx.flush()
        fintok = ctx.tok()

        def f_fin(e):
            ctx.wait_tok(e, "sync", fintok)

        ctx.emit(sync=f_fin)
        ctx.flush()

    return nc


def _pack_cols(WT, G):
    Din, Dout = WT.shape
    nD, nG = Din // 128, Dout // G
    R = WT.reshape(nD, 128, nG, G).transpose(2, 1, 0, 3)
    return np.ascontiguousarray(R).reshape(nG * 128, nD * G)


def _host_inputs(cfg, x, Wq, Wk, Wv, Wg, Wp, W_g, W_u, W_d, n_cores=8):
    tr = lambda w: np.ascontiguousarray(np.asarray(w, dtype=np.float32).T)
    wqT = _pack_cols(tr(Wq), cfg.OG)
    wkT = _pack_cols(tr(Wk), cfg.OG)
    wvT = _pack_cols(tr(Wv), cfg.OG)
    wgT = _pack_cols(tr(Wg), cfg.OG)
    wpT = _pack_cols(tr(Wp), cfg.DOS)
    fgT = _pack_cols(tr(W_g), 128)
    fuT = _pack_cols(tr(W_u), 128)
    fdT = tr(W_d)
    x = np.asarray(x, dtype=np.float32)
    nquart = n_cores // x.shape[0]
    in_maps = []
    for c in range(n_cores):
        b, q = c // nquart, c % nquart
        pw = np.zeros((1, 4), np.float32)
        pw[0, :q] = 1.0
        in_maps.append(dict(
            x0=np.ascontiguousarray(x[b, q * cfg.T:(q + 1) * cfg.T, :]),
            wqT=wqT, wkT=wkT, wvT=wvT, wgT=wgT, wpT=wpT,
            fgT=fgT, fuT=fuT, fdT=fdT, prefw=pw))
    return in_maps


def kernel(L0_attn_x, L0_attn_Wq, L0_attn_Wk, L0_attn_Wv, L0_attn_Wg, L0_attn_Wp,
           L0_ffn_W_g, L0_ffn_W_u, L0_ffn_W_d):
    cfg = Cfg()
    nc = build_nc(cfg)
    in_maps = _host_inputs(cfg, L0_attn_x, L0_attn_Wq, L0_attn_Wk, L0_attn_Wv,
                           L0_attn_Wg, L0_attn_Wp, L0_ffn_W_g, L0_ffn_W_u, L0_ffn_W_d)
    res = run_bass_kernel_spmd(nc, in_maps, core_ids=list(range(8)))
    out = np.empty((2, 4096, 2048), np.float32)
    for c in range(8):
        b, q = c // 4, c % 4
        out[b, q * 1024:(q + 1) * 1024, :] = res.results[c]["out"]
    return out


# version 31
# speedup vs baseline: 1.0184x; 1.0184x over previous
import contextlib
import numpy as np

import concourse.bass as bass
import concourse.mybir as mybir
from concourse.bass_utils import run_bass_kernel_spmd
from concourse.alu_op_type import AluOpType

F32 = mybir.dt.float32
BF16 = mybir.dt.bfloat16
AF = mybir.ActivationFunctionType
AX = mybir.AxisListType
A = AluOpType

M32C = 12582912.0
M64C = 6755399441055744.0
EPS_LN = 1e-5
EPS_FFN = 1e-8
EPS_RMS = 1.1920928955078125e-07

ENG = ("sync", "tensor", "vector", "scalar", "gpsimd")


class Cfg:
    def __init__(self, T=1024, D=2048, H=8192, phases=("ffn1", "attn", "ffn2")):
        self.T, self.D, self.H = T, D, H
        self.BS = 256
        self.phases = phases
        assert T % self.BS == 0 and D % 512 == 0 and H % 128 == 0
        self.nT = T // 128
        self.nD = D // 128
        self.nH = H // 128
        self.nC = T // self.BS
        self.TG = min(512, T)
        self.nTG = T // self.TG
        self.SPAN = min(1024, H)
        self.nSPAN = H // self.SPAN
        self.spanH = self.SPAN // 128
        self.DOS = min(512, D)
        self.nDO = D // self.DOS
        self.OG = min(128, D)
        self.nOG = D // self.OG


FLUSH_N = 512


def _blk(nc, **fns):
    with nc.Block(no_gpsimd_drain=True) as blk:
        for name, fn in fns.items():
            getattr(blk, name)(fn)


class Ctx:
    def __init__(self, nc, es):
        self.nc = nc
        self.sem = {}
        self.cnt = {}
        self.mark = {e: {} for e in ENG}
        self.cur_blk = None
        self.blk_n = 0
        for name in ("qs", "qg", "te", "ve", "se", "gp", "cc", "qsp", "qst",
                     "qxc", "qxd",
                     "qa", "qb", "qa1", "qb1", "qw", "qw0", "qw1", "qwd",
                     "qc0", "qc1", "qga", "qgb",
                     "qsa", "qsb", "qxa", "qxb", "qy0", "qy1"):
            self.sem[name] = es.enter_context(nc.semaphore(name))
            self.cnt[name] = 0

    def emit(self, **fns):
        if self.cur_blk is None:
            self.cur_blk = self.nc.Block(no_gpsimd_drain=True)
            self.cur_blk.__enter__()
            self.blk_n = 0
        for name, fn in fns.items():
            getattr(self.cur_blk, name)(fn)
        self.blk_n += 1
        if self.blk_n >= FLUSH_N:
            self.flush()

    def flush(self):
        if self.cur_blk is not None:
            self.cur_blk.__exit__(None, None, None)
            self.cur_blk = None

    def inc(self, instr, sname, amt=1):
        instr.then_inc(self.sem[sname], amt)
        self.cnt[sname] += amt
        return self.cnt[sname]

    def wait(self, e, ename, sname, val=None):
        if val is None:
            val = self.cnt[sname]
        if val <= 0:
            return
        wm = self.mark[ename]
        if wm.get(sname, 0) >= val:
            return
        e.wait_ge(self.sem[sname], val)
        wm[sname] = val

    def tok(self):
        return dict(self.cnt)

    def wait_tok(self, e, ename, token):
        for sname, val in token.items():
            self.wait(e, ename, sname, val)

    def barrier(self, nc):
        self.flush()
        t = self.tok()
        t.pop("cc", None)
        fns = {}
        for ename in ENG:
            fns[ename] = (lambda en: lambda e: self.wait_tok(e, en, t))(ename)
        _blk(nc, **fns)


def build_nc(cfg: Cfg):
    nc = bass.Bass("TRN2", target_bir_lowering=False)
    T, D, H = cfg.T, cfg.D, cfg.H
    nT, nD, nC = cfg.nT, cfg.nD, cfg.nC

    x0_t = nc.dram_tensor("x0", [T, D], F32, kind="ExternalInput")
    OGp, nOGp = cfg.OG, cfg.nOG
    wqT_t = nc.dram_tensor("wqT", [nOGp * 128, cfg.nD * OGp], F32, kind="ExternalInput")
    wkT_t = nc.dram_tensor("wkT", [nOGp * 128, cfg.nD * OGp], F32, kind="ExternalInput")
    wvT_t = nc.dram_tensor("wvT", [nOGp * 128, cfg.nD * OGp], F32, kind="ExternalInput")
    wgT_t = nc.dram_tensor("wgT", [nOGp * 128, cfg.nD * OGp], F32, kind="ExternalInput")
    wpT_t = nc.dram_tensor("wpT", [cfg.nDO * 128, cfg.nD * cfg.DOS], F32, kind="ExternalInput")
    fgT_t = nc.dram_tensor("fgT", [cfg.nH * 128, cfg.nD * 128], F32, kind="ExternalInput")
    fuT_t = nc.dram_tensor("fuT", [cfg.nH * 128, cfg.nD * 128], F32, kind="ExternalInput")
    fdT_t = nc.dram_tensor("fdT", [H, D], F32, kind="ExternalInput")
    pw_t = nc.dram_tensor("prefw", [1, 4], F32, kind="ExternalInput")
    out_t = nc.dram_tensor("out", [T, D], F32, kind="ExternalOutput")

    aqC_t = nc.dram_tensor("aqC", [nOGp * 128, cfg.nD * OGp], BF16)
    akC_t = nc.dram_tensor("akC", [nOGp * 128, cfg.nD * OGp], BF16)
    avC_t = nc.dram_tensor("avC", [nOGp * 128, cfg.nD * OGp], BF16)
    agC_t = nc.dram_tensor("agC", [nOGp * 128, cfg.nD * OGp], BF16)
    apC_t = nc.dram_tensor("apC", [cfg.nDO * 128, cfg.nD * cfg.DOS], BF16)
    sgC_t = nc.dram_tensor("sgC", [cfg.nH * 128, cfg.nD * 128], BF16)
    suC_t = nc.dram_tensor("suC", [cfg.nH * 128, cfg.nD * 128], BF16)
    sdC_t = nc.dram_tensor("sdC", [H, D], BF16)
    x1_t = nc.dram_tensor("x1", [T, D], F32)
    x2_t = nc.dram_tensor("x2", [T, D], F32)
    xnT_t = nc.dram_tensor("xnT_spill", [D, T], BF16)
    HD = cfg.nD * D // 2
    cc_in_h = [nc.dram_tensor(f"cc_in{i}", [128, HD], BF16) for i in range(2)]
    cc_out_h = [nc.dram_tensor(f"cc_out{i}", [4 * 128, HD], BF16) for i in range(2)]

    es = contextlib.ExitStack()
    with es:
        ctx = Ctx(nc, es)

        _uniq = [0]

        def sb(stk, name, shape, dt, side=None):
            ctx.flush()
            _uniq[0] += 1
            if side is None:
                return stk.enter_context(nc.sbuf_tensor(f"{name}_{_uniq[0]}", shape, dt))
            return stk.enter_context(nc.sbuf_tensor(f"{name}_{_uniq[0]}", shape, dt,
                                                    side=side))

        def psb(stk, name, shape, dt):
            ctx.flush()
            _uniq[0] += 1
            return stk.enter_context(nc.psum_tensor(f"{name}_{_uniq[0]}", shape, dt))

        cast_pieces = []
        cast_tok = {}

        def emit_cast_piece(n=1):
            for _ in range(n):
                if not cast_pieces:
                    return
                kind, srcw, dstw, r0, rn = cast_pieces.pop(0)

                def f_cp(e, kind=kind, srcw=srcw, dstw=dstw, r0=r0, rn=rn):
                    ctx.inc(e.dma_start(out=dstw.ap()[r0:r0 + rn, :],
                                        in_=srcw.ap()[r0:r0 + rn, :]), "qg", 16)
                    cast_tok[kind] = ctx.cnt["qg"]

                ctx.emit(gpsimd=f_cp)

        ident32 = sb(es, "ident32", [128, 128], F32)
        identb = sb(es, "identb", [128, 128], BF16)
        maskT = [sb(es, "maskTa", [128, 256], BF16), sb(es, "maskTb", [128, 256], BF16)]
        magic = sb(es, "magic", [128, 1], F32)
        prefw = sb(es, "prefw_sb", [128, 4], F32)
        sAp = [sb(es, f"sA{i}", [128, 1], F32) for i in range(4)]
        sBp = [sb(es, f"sB{i}", [128, 1], F32) for i in range(4)]
        sCp = [sb(es, f"sC{i}", [128, 1], F32) for i in range(4)]
        sA, sB, sC = sAp[0], sBp[0], sCp[0]

        def emit_consts():
            with contextlib.ExitStack() as stk:
                io_r = sb(stk, "io_r", [128, 256], F32)
                io_c = sb(stk, "io_c", [128, 256], F32)

                def f_sync(e):
                    pass

                def f_gp(e):
                    ctx.inc(e.iota(io_r[:], [[0, 256]], channel_multiplier=1,
                                   allow_small_or_imprecise_dtypes=True), "gp")
                    ctx.inc(e.iota(io_c[:], [[1, 256]], channel_multiplier=0,
                                   allow_small_or_imprecise_dtypes=True), "gp")
                    ctx.inc(e.dma_start(out=prefw[:],
                                        in_=bass.AP(pw_t, 0, [[0, 128], [1, 4]])), "qg", 16)

                def f_ve(e):
                    ctx.wait(e, "vector", "gp")
                    ctx.inc(e.tensor_tensor(out=ident32[:], in0=io_r[:, 0:128],
                                            in1=io_c[:, 0:128], op=A.is_equal), "ve")
                    ctx.inc(e.tensor_tensor(out=identb[:], in0=io_r[:, 0:128],
                                            in1=io_c[:, 0:128], op=A.is_equal), "ve")
                    ctx.inc(e.tensor_tensor(out=maskT[0][:], in0=io_c[:], in1=io_r[:],
                                            op=A.is_ge), "ve")
                    ctx.wait(e, "vector", "ve")
                    ctx.inc(e.tensor_scalar(out=io_r[:], in0=io_r[:], scalar1=128.0,
                                            scalar2=None, op0=A.add), "ve")
                    ctx.wait(e, "vector", "ve")
                    ctx.inc(e.tensor_tensor(out=maskT[1][:], in0=io_c[:], in1=io_r[:],
                                            op=A.is_ge), "ve")
                    ctx.inc(e.memset(sA[:], 1e-9), "ve")
                    ctx.wait(e, "vector", "ve")
                    ctx.inc(e.tensor_scalar(out=sA[:], in0=sA[:], scalar1=1.0, scalar2=1.0,
                                            op0=A.add, op1=A.is_gt), "ve")
                    ctx.wait(e, "vector", "ve")
                    ctx.inc(e.tensor_scalar(out=magic[:], in0=sA[:], scalar1=M64C - M32C,
                                            scalar2=M32C, op0=A.mult, op1=A.add), "ve")

                ctx.emit(sync=f_sync, gpsimd=f_gp, vector=f_ve)
                ctx.barrier(nc)

        def emit_stage_a(src_t, dstT, psTfp, zscAp, zscBp, xtile_fn, attn_mode, spill):
            ve_done = {}
            se_done = {}
            psTf_evac = [0, 0]
            st = {}
            NPAR = len(zscAp)

            def tile_steps(tt, se_ops):
                par = tt % NPAR
                sA, sB, sC = sAp[par], sBp[par], sCp[par]
                zscA, zscB = zscAp[par], zscBp[par]
                xt = xtile_fn(tt)
                qld = ("qxa", "qxb", "qxc", "qxd")[par]
                steps = []

                def s_load():
                    def f_sync(e):
                        ctx.wait(e, "sync", "ve", ve_done.get(tt - NPAR, 0))
                        ctx.wait(e, "sync", "se", se_done.get(tt - NPAR, 0))
                        ctx.inc(e.dma_start(out=xt,
                                            in_=src_t.ap()[tt * 128:(tt + 1) * 128, :]),
                                qld, 16)
                    ctx.emit(sync=f_sync)
                    st[(tt, "ld")] = ctx.cnt[qld]
                steps.append(s_load)

                def sq1_op(e):
                    ctx.wait(e, "scalar", qld, st[(tt, "ld")])
                    ctx.wait(e, "scalar", "ve", st.get((tt - NPAR, "zA_done"), 0))
                    n = ctx.inc(e.activation(out=zscA[:], in_=xt, func=AF.Square,
                                             accum_out=sB[:]), "se")
                    assert n == st[(tt, "sq")]
                se_ops[0].append(sq1_op)

                def s_stats1():
                    def f_ve(e):
                        ctx.wait(e, "vector", qld, st[(tt, "ld")])
                        ctx.inc(e.reduce_sum(out=sA[:], in_=xt, axis=AX.X), "ve")
                    ctx.emit(vector=f_ve)
                    st[(tt, "r1")] = ctx.cnt["ve"]
                steps.append(s_stats1)

                def s_comb1():
                    def f_ve(e):
                        ctx.wait(e, "vector", "ve", st[(tt, "r1")])
                        ctx.wait(e, "vector", "se", st[(tt, "sq")])
                        ctx.inc(e.tensor_tensor(out=sC[:], in0=sA[:], in1=sA[:],
                                                op=A.mult), "ve")
                        ctx.wait(e, "vector", "ve")
                        ctx.inc(e.scalar_tensor_tensor(out=sC[:], in0=sC[:],
                                                       scalar=-1.0 / D, in1=sB[:],
                                                       op0=A.mult, op1=A.add), "ve")
                    ctx.emit(vector=f_ve)
                    st[(tt, "w")] = ctx.cnt["ve"]
                steps.append(s_comb1)

                def sqrt1_op(e):
                    ctx.wait(e, "scalar", "ve", st[(tt, "wv")])
                    n = ctx.inc(e.activation(out=sC[:], in_=sC[:], func=AF.Sqrt), "se")
                    assert n == st[(tt, "std")]
                se_ops[1].append(sqrt1_op)

                def s_sqrt1():
                    def f_ve(e):
                        ctx.wait(e, "vector", "ve", st[(tt, "w")])
                        ctx.inc(e.tensor_scalar(out=sC[:], in0=sC[:], scalar1=1.0 / D,
                                                scalar2=EPS_LN, op0=A.mult, op1=A.add),
                                "ve")
                    ctx.emit(vector=f_ve)
                    st[(tt, "wv")] = ctx.cnt["ve"]
                steps.append(s_sqrt1)

                def s_apply1():
                    def f_ve(e):
                        ctx.wait(e, "vector", "se", st[(tt, "std")])
                        ctx.inc(e.reciprocal(out=sC[:], in_=sC[:]), "ve")
                        ctx.wait(e, "vector", "ve")
                        ctx.inc(e.scalar_tensor_tensor(out=sA[:], in0=sA[:],
                                                       scalar=1.0 / D, in1=sC[:],
                                                       op0=A.mult, op1=A.mult), "ve")
                        ctx.wait(e, "vector", "ve")
                        ctx.inc(e.tensor_scalar(out=zscA[:], in0=xt, scalar1=sC[:, 0:1],
                                                scalar2=sA[:, 0:1], op0=A.mult,
                                                op1=A.subtract), "ve")
                    ctx.emit(vector=f_ve)
                    st[(tt, "y")] = ctx.cnt["ve"]
                steps.append(s_apply1)

                if not attn_mode:
                    def sq2_op(e):
                        ctx.wait(e, "scalar", "ve", st[(tt, "y")])
                        n = ctx.inc(e.activation(out=zscB[:], in_=zscA[:],
                                                 func=AF.Square, accum_out=sB[:]), "se")
                        assert n == st[(tt, "sq2")]
                    se_ops[2].append(sq2_op)

                    def s_stats2():
                        def f_ve(e):
                            ctx.wait(e, "vector", "ve", st[(tt, "y")])
                            ctx.inc(e.reduce_sum(out=sA[:], in_=zscA[:], axis=AX.X), "ve")
                        ctx.emit(vector=f_ve)
                        st[(tt, "r2")] = ctx.cnt["ve"]
                    steps.append(s_stats2)

                    def s_comb2():
                        def f_ve(e):
                            ctx.wait(e, "vector", "ve", st[(tt, "r2")])
                            ctx.wait(e, "vector", "se", st[(tt, "sq2")])
                            ctx.inc(e.tensor_tensor(out=sC[:], in0=sA[:], in1=sA[:],
                                                    op=A.mult), "ve")
                            ctx.wait(e, "vector", "ve")
                            ctx.inc(e.scalar_tensor_tensor(out=sC[:], in0=sC[:],
                                                           scalar=-1.0 / D, in1=sB[:],
                                                           op0=A.mult, op1=A.add), "ve")
                        ctx.emit(vector=f_ve)
                        st[(tt, "w2")] = ctx.cnt["ve"]
                    steps.append(s_comb2)

                    def sqrt2_op(e):
                        ctx.wait(e, "scalar", "ve", st[(tt, "w2v")])
                        n = ctx.inc(e.activation(out=sC[:], in_=sC[:], func=AF.Sqrt),
                                    "se")
                        assert n == st[(tt, "std2")]
                    se_ops[3].append(sqrt2_op)

                    def s_sqrt2():
                        def f_ve(e):
                            ctx.wait(e, "vector", "ve", st[(tt, "w2")])
                            ctx.inc(e.tensor_scalar(out=sC[:], in0=sC[:], scalar1=1.0 / D,
                                                    scalar2=EPS_FFN, op0=A.mult,
                                                    op1=A.add), "ve")
                        ctx.emit(vector=f_ve)
                        st[(tt, "w2v")] = ctx.cnt["ve"]
                    steps.append(s_sqrt2)

                    def s_apply2():
                        def f_ve(e):
                            ctx.wait(e, "vector", "se", st[(tt, "std2")])
                            ctx.inc(e.reciprocal(out=sC[:], in_=sC[:]), "ve")
                            ctx.wait(e, "vector", "ve")
                            ctx.inc(e.scalar_tensor_tensor(out=sA[:], in0=sA[:],
                                                           scalar=1.0 / D, in1=sC[:],
                                                           op0=A.mult, op1=A.mult), "ve")
                            ctx.wait(e, "vector", "ve")
                            ctx.inc(e.tensor_scalar(out=zscB[:], in0=zscA[:],
                                                    scalar1=sC[:, 0:1], scalar2=sA[:, 0:1],
                                                    op0=A.mult, op1=A.subtract), "ve")
                        ctx.emit(vector=f_ve)
                        st[(tt, "z")] = ctx.cnt["ve"]
                    steps.append(s_apply2)

                    def s_amax():
                        def f_ve(e):
                            ctx.wait(e, "vector", "ve", st[(tt, "z")])
                            ctx.inc(e.reduce_max(out=sA[:], in_=zscB[:], axis=AX.X,
                                                 apply_absolute_value=True), "ve")
                            ctx.wait(e, "vector", "ve")
                            ctx.inc(e.reciprocal(out=sB[:], in_=sA[:]), "ve")
                            ctx.wait(e, "vector", "ve")
                            ctx.inc(e.tensor_scalar(out=sB[:], in0=sB[:], scalar1=127.0,
                                                    scalar2=None, op0=A.mult), "ve")
                            ctx.wait(e, "vector", "ve")
                            ctx.inc(e.reciprocal(out=sC[:], in_=sB[:]), "ve")
                        ctx.emit(vector=f_ve)
                        st[(tt, "sinv")] = ctx.cnt["ve"]
                    steps.append(s_amax)

                    def s_round():
                        def f_ve(e):
                            ctx.wait(e, "vector", "ve", st[(tt, "sinv")])
                            ctx.inc(e.tensor_scalar(out=zscA[:], in0=zscB[:],
                                                    scalar1=sB[:, 0:1],
                                                    scalar2=magic[:, 0:1], op0=A.mult,
                                                    op1=A.add), "ve")
                            ctx.wait(e, "vector", "ve")
                            ctx.inc(e.tensor_scalar(out=zscB[:], in0=zscA[:],
                                                    scalar1=magic[:, 0:1],
                                                    scalar2=sC[:, 0:1], op0=A.subtract,
                                                    op1=A.mult), "ve")
                        ctx.emit(vector=f_ve)
                        st[(tt, "zB")] = ctx.cnt["ve"]
                        st[(tt, "zA_done")] = ctx.cnt["ve"]
                    steps.append(s_round)
                else:
                    def sq2_op(e):
                        ctx.wait(e, "scalar", "ve", st[(tt, "y")])
                        n = ctx.inc(e.activation(out=zscB[:], in_=zscA[:],
                                                 func=AF.Square, accum_out=sB[:]), "se")
                        assert n == st[(tt, "sq2")]
                    se_ops[2].append(sq2_op)

                    def sqrt2_op(e):
                        ctx.wait(e, "scalar", "ve", st[(tt, "w2v")])
                        n = ctx.inc(e.activation(out=sB[:], in_=sB[:], func=AF.Sqrt),
                                    "se")
                        assert n == st[(tt, "std2")]
                    se_ops[3].append(sqrt2_op)

                    def s_rms():
                        def f_ve2(e):
                            ctx.wait(e, "vector", "se", st[(tt, "sq2")])
                            ctx.inc(e.tensor_scalar(out=sB[:], in0=sB[:], scalar1=1.0 / D,
                                                    scalar2=EPS_RMS, op0=A.mult,
                                                    op1=A.add), "ve")
                        ctx.emit(vector=f_ve2)
                        st[(tt, "w2v")] = ctx.cnt["ve"]
                    steps.append(s_rms)

                    def s_rmsapply():
                        def f_ve(e):
                            ctx.wait(e, "vector", "se", st[(tt, "std2")])
                            ctx.inc(e.reciprocal(out=sB[:], in_=sB[:]), "ve")
                            ctx.wait(e, "vector", "ve")
                            ctx.inc(e.tensor_scalar(out=zscB[:], in0=zscA[:],
                                                    scalar1=sB[:, 0:1], scalar2=None,
                                                    op0=A.mult), "ve")
                        ctx.emit(vector=f_ve)
                        st[(tt, "zB")] = ctx.cnt["ve"]
                        st[(tt, "zA_done")] = ctx.cnt["ve"]
                    steps.append(s_rmsapply)

                def s_transpose():
                    for ri, r0 in enumerate(range(0, nD, 4)):
                        rn = min(4, nD - r0)
                        pfi = (tt * ((nD + 3) // 4) + ri) % 2
                        pf = psTfp[pfi]

                        def f_te(e, r0=r0, rn=rn, pf=pf, pfi=pfi):
                            ctx.wait(e, "tensor", "ve", st[(tt, "zB")])
                            ctx.wait(e, "tensor", "ve", psTf_evac[pfi])
                            for k in range(rn):
                                ctx.inc(e.transpose(
                                    pf[:, k * 128:(k + 1) * 128],
                                    zscB[:, (r0 + k) * 128:(r0 + k + 1) * 128],
                                    ident32[:]), "te")

                        ctx.emit(tensor=f_te)
                        trtok = ctx.cnt["te"]

                        def f_ve9(e, r0=r0, rn=rn, pf=pf, pfi=pfi, trtok=trtok):
                            ctx.wait(e, "vector", "te", trtok)
                            ctx.inc(e.tensor_copy(
                                out=bass.AP(dstT, r0 * T + tt * 128,
                                            [[nD * T, 128], [T, rn], [1, 128]]),
                                in_=pf[:, 0:rn * 128]), "ve")
                            psTf_evac[pfi] = ctx.cnt["ve"]

                        ctx.emit(vector=f_ve9)

                    if spill:
                        def f_sp(e):
                            ctx.wait(e, "sync", "ve")
                            ctx.inc(e.dma_start(
                                out=bass.AP(xnT_t, tt * 128,
                                            [[T, 128], [128 * T, nD], [1, 128]]),
                                in_=bass.AP(dstT, tt * 128,
                                            [[nD * T, 128], [T, nD], [1, 128]])),
                                "qsp", 16)
                        ctx.emit(sync=f_sp)
                    ve_done[tt] = ctx.cnt["ve"]
                    se_done[tt] = ctx.cnt["se"]
                steps.append(s_transpose)
                return steps

            for pair in range(0, nT, NPAR):
                tiles = [tt for tt in range(pair, min(pair + NPAR, nT))]
                k = len(tiles)
                se0 = ctx.cnt["se"]
                for i, tt in enumerate(tiles):
                    st[(tt, "sq")] = se0 + 0 * k + i + 1
                    st[(tt, "std")] = se0 + 1 * k + i + 1
                    st[(tt, "sq2")] = se0 + 2 * k + i + 1
                    st[(tt, "std2")] = se0 + 3 * k + i + 1
                se_ops = [[] for _ in range(4)]
                stepl = [tile_steps(tt, se_ops) for tt in tiles]
                for i in range(max(len(s) for s in stepl)):
                    for s in stepl:
                        if i < len(s):
                            s[i]()

                def f_sem(e, ops=se_ops):
                    for row in ops:
                        for fn in row:
                            fn(e)

                ctx.emit(scalar=f_sem)
                assert ctx.cnt["se"] == se0 + 4 * k
                for i, tt in enumerate(tiles):
                    se_done[tt] = se0 + i + 1

        def emit_ffn(src_t, dst_t, cache=None):
            fs = contextlib.ExitStack()
            with fs:
                acc = sb(fs, "acc", [128, nT * D], F32)
                xqT = sb(fs, "xqT", [128, nD * T], BF16)

                spanH, SPAN, nSPAN = cfg.spanH, cfg.SPAN, cfg.nSPAN
                TG, nTG = cfg.TG, cfg.nTG
                DOS, nDO = cfg.DOS, cfg.nDO
                sk1 = fs.enter_context(contextlib.ExitStack())
                wgb = [sb(sk1, f"wgb{i}", [128, nD * 128], BF16) for i in range(2)]
                wub = [sb(sk1, f"wub{i}", [128, nD * 128], BF16) for i in range(2)]
                if cache != "read":
                    wst = [sb(sk1, f"wst{i}", [128, nD * 128], F32) for i in range(2)]
                    wstu = [sb(sk1, f"wstu{i}", [128, nD * 128], F32) for i in range(2)]

                if True:

                    gu_sign = {}
                    ustg_free = {}
                    gu_mm = {}
                    wd_sign = {}
                    wd_mm = {}
                    pT_mult = {}
                    psgu_evac2 = [[0, 0], [0, 0]]
                    stg_free = {}
                    psgu_evac = [0, 0]
                    psA_evac = [0, 0]

                    def fgT_ap(ht):
                        return bass.AP(fgT_t, ht * 128 * nD * 128, [[nD * 128, 128], [1, nD * 128]])

                    def fuT_ap(ht):
                        return bass.AP(fuT_t, ht * 128 * nD * 128, [[nD * 128, 128], [1, nD * 128]])

                    def load_gu(ht, sgn=None, post=None):
                        b = ht % 2
                        qag = "qa" if b == 0 else "qa1"
                        qbg = "qb" if b == 0 else "qb1"

                        if cache == "read":
                            def f_sync(e, ht=ht, b=b, qag=qag, qbg=qbg):
                                if ht >= 2:
                                    ctx.wait(e, "sync", "te", gu_mm[ht - 2])
                                ctx.inc(e.dma_start(
                                    out=wgb[b][:],
                                    in_=sgC_t.ap()[ht * 128:(ht + 1) * 128, :]), qag, 16)
                                ctx.inc(e.dma_start(
                                    out=wub[b][:],
                                    in_=suC_t.ap()[ht * 128:(ht + 1) * 128, :]), qbg, 16)
                                gu_sign[ht] = ((qag, ctx.cnt[qag]), (qbg, ctx.cnt[qbg]))

                            ctx.emit(sync=f_sync)
                            return

                        def f_sync(e, ht=ht, b=b, qag=qag):
                            if ht >= 2:
                                ctx.wait(e, "sync", "se", stg_free[ht - 2])
                                ctx.wait(e, "sync", "te", gu_mm[ht - 2])
                            ctx.inc(e.dma_start(out=wst[b][:], in_=fgT_ap(ht)), qag, 16)

                        def f_se(e, ht=ht, b=b, qag=qag):
                            ctx.wait(e, "scalar", qag)
                            if cache == "write":
                                ctx.wait(e, "scalar", f"qw{b}")
                            ctx.inc(e.activation(out=wgb[b][:], in_=wst[b][:], func=AF.Sign),
                                    "se")
                            stg_free[ht] = ctx.cnt["se"]

                        def f_sync2(e, ht=ht, b=b, qbg=qbg):
                            if ht >= 2:
                                ctx.wait(e, "sync", "se", ustg_free[ht - 2])
                            ctx.inc(e.dma_start(out=wstu[b][:], in_=fuT_ap(ht)), qbg, 16)

                        def f_se2(e, ht=ht, b=b, qbg=qbg):
                            ctx.wait(e, "scalar", qbg)
                            if cache == "write":
                                ctx.wait(e, "scalar", f"qw{b}")
                            ctx.inc(e.activation(out=wub[b][:], in_=wstu[b][:], func=AF.Sign),
                                    "se")
                            gu_sign[ht] = (("se", ctx.cnt["se"]),)
                            stg_free[ht] = ctx.cnt["se"]
                            ustg_free[ht] = ctx.cnt["se"]

                        def emit_wr():
                            gtok = ctx.cnt["se"]

                            def f_wr(e, ht=ht, b=b, gtok=gtok):
                                ctx.wait(e, "gpsimd", "se", gtok)
                                ctx.inc(e.dma_start(
                                    out=sgC_t.ap()[ht * 128:(ht + 1) * 128, :],
                                    in_=wgb[b][:]), f"qw{b}", 16)
                                ctx.inc(e.dma_start(
                                    out=suC_t.ap()[ht * 128:(ht + 1) * 128, :],
                                    in_=wub[b][:]), f"qw{b}", 16)

                            ctx.emit(gpsimd=f_wr)

                        if sgn is None:
                            ctx.emit(sync=f_sync, scalar=f_se)
                            ctx.emit(sync=f_sync2, scalar=f_se2)
                            if cache == "write":
                                emit_wr()
                        else:
                            ctx.emit(sync=f_sync)
                            ctx.emit(sync=f_sync2)
                            sgn.extend([f_se, f_se2])
                            if cache == "write":
                                post.append(emit_wr)

                    def load_wd(sp):
                        if cache == "read":
                            def f_sync(e, sp=sp):
                                if sp >= 1:
                                    ctx.wait(e, "sync", "te", wd_mm[sp - 1])
                                ctx.inc(e.dma_start(
                                    out=wdb[:],
                                    in_=bass.AP(sdC_t, sp * SPAN * D,
                                                [[D, 128], [128 * D, spanH], [1, D]])),
                                    "qc0", 16)
                                wd_sign[sp] = (("qc0", ctx.cnt["qc0"]),)

                            ctx.emit(sync=f_sync)
                            return
                        raise AssertionError("write-mode load_wd replaced by load_wd_piece")

                    wd_pc_sign = {}

                    def load_wd_piece(sp, pc, sgn=None, post=None):
                        def f_sync(e, sp=sp, pc=pc):
                            if pc >= 1:
                                ctx.wait(e, "sync", "se", wd_pc_sign[pc - 1])
                            if pc == 0 and sp >= 1:
                                ctx.wait(e, "sync", "te", wd_mm[sp - 1])
                            ctx.inc(e.dma_start(
                                out=wdst[0][:],
                                in_=bass.AP(fdT_t, (sp * SPAN + pc * 128) * D,
                                            [[D, 128], [1, D]])), "qc0", 16)

                        def f_se(e, sp=sp, pc=pc):
                            ctx.wait(e, "scalar", "qc0")
                            if pc == 0 and sp >= 1:
                                ctx.wait(e, "scalar", "te", wd_mm[sp - 1])
                            if cache == "write":
                                ctx.wait(e, "scalar", "qwd")
                            ctx.inc(e.activation(out=wdb[:, pc * D:(pc + 1) * D],
                                                 in_=wdst[0][:], func=AF.Sign), "se")
                            wd_pc_sign[pc] = ctx.cnt["se"]
                            wd_sign[sp] = (("se", ctx.cnt["se"]),)

                        def emit_wr():
                            ptok = ctx.cnt["se"]

                            def f_wr(e, sp=sp, pc=pc, ptok=ptok):
                                ctx.wait(e, "gpsimd", "se", ptok)
                                ctx.inc(e.dma_start(
                                    out=bass.AP(sdC_t, (sp * SPAN + pc * 128) * D,
                                                [[D, 128], [1, D]]),
                                    in_=wdb[:, pc * D:(pc + 1) * D]), "qwd", 16)

                            ctx.emit(gpsimd=f_wr)

                        if sgn is None:
                            ctx.emit(sync=f_sync, scalar=f_se)
                            if cache == "write":
                                emit_wr()
                        else:
                            ctx.emit(sync=f_sync)
                            sgn.append(f_se)
                            if cache == "write":
                                post.append(emit_wr)

                    load_gu(0)
                    if cfg.nH > 1:
                        load_gu(1)

                    with contextlib.ExitStack() as sa:
                        nsa = 4 if cache == "read" else 2
                        zscAp = [sb(sa, f"zscA{i}", [128, D], F32) for i in range(nsa)]
                        zscBp = [sb(sa, f"zscB{i}", [128, D], F32) for i in range(nsa)]
                        psTfp = [psb(sa, f"psTf{i}", [128, 512], F32) for i in range(2)]
                        emit_stage_a(src_t, xqT, psTfp, zscAp, zscBp,
                                     lambda tt: acc[:, tt * D:(tt + 1) * D],
                                     attn_mode=False, spill=False)
                        ctx.barrier(nc)

                    sk = fs.enter_context(contextlib.ExitStack())
                    pT = sb(sk, "pT", [128, spanH * T], BF16)
                    wdb = sb(sk, "wdb", [128, spanH * D], BF16)
                    if cache != "read":
                        wdst = [sb(sk, "wdst0", [128, D], F32)]
                    gsc = sb(sk, "gsc", [128, T], BF16)
                    usc = sb(sk, "usc", [128, T], BF16)
                    psg = [psb(sk, f"psg{i}", [128, 512], F32) for i in range(nTG)]
                    psu = [psb(sk, f"psu{i}", [128, 512], F32) for i in range(nTG)]
                    DH = D // 2
                    DOSd = min(DOS, DH)
                    psA2 = [psb(sk, f"psA{i}", [128, DH], F32) for i in range(2)]

                    def down_proj(sp):
                        for tt in range(nT):
                            for half in range(2):
                                def f_te2(e, sp=sp, tt=tt, half=half):
                                    ctx.wait(e, "tensor", "ve",
                                             pT_mult[sp * spanH + spanH - 1])
                                    for sname, cval in wd_sign[sp]:
                                        ctx.wait(e, "tensor", sname, cval)
                                    ctx.wait(e, "tensor", "ve", psA_evac[half])
                                    ctx.wait(e, "tensor", "ve", psgu_evac2[1][0])
                                    ctx.wait(e, "tensor", "se", psgu_evac2[1][1])
                                    for hti in range(spanH):
                                        lhs = pT[:, hti * T + tt * 128:
                                                 hti * T + (tt + 1) * 128]
                                        for dsl in range(DH // DOSd):
                                            ds = half * (DH // DOSd) + dsl
                                            ctx.inc(e.matmul(
                                                psA2[half][:, dsl * DOSd:
                                                           (dsl + 1) * DOSd],
                                                lhs,
                                                wdb[:, hti * D + ds * DOSd:
                                                    hti * D + (ds + 1) * DOSd],
                                                start=(hti == 0),
                                                stop=(hti == spanH - 1)), "te")

                                ctx.emit(tensor=f_te2)
                                dn_tok = ctx.cnt["te"]

                                def f_ve4(e, tt=tt, half=half, dn_tok=dn_tok):
                                    ctx.wait(e, "vector", "te", dn_tok)
                                    ctx.inc(e.tensor_tensor(
                                        out=acc[:, tt * D + half * DH:
                                                tt * D + (half + 1) * DH],
                                        in0=psA2[half][:, 0:DH],
                                        in1=acc[:, tt * D + half * DH:
                                                tt * D + (half + 1) * DH],
                                        op=A.add), "ve")
                                    psA_evac[half] = ctx.cnt["ve"]

                                ctx.emit(vector=f_ve4)
                        wd_mm[sp] = ctx.cnt["te"]

                    for sp in range(nSPAN):
                        if cache == "read" and sp == 0:
                            load_wd(sp)
                        for hti in range(spanH):
                            ht = sp * spanH + hti
                            b = ht % 2

                            slot = hti % 2 if nTG == 2 else 0
                            if slot == 0:
                                pgl = [psg[tg][:, 0:TG] for tg in range(nTG)]
                                pul = [psu[tg][:, 0:TG] for tg in range(nTG)]
                            else:
                                pgl = [psA2[0][:, 0:TG], psA2[0][:, TG:2 * TG]]
                                pul = [psA2[1][:, 0:TG], psA2[1][:, TG:2 * TG]]

                            def f_te(e, ht=ht, b=b, slot=slot, pgl=pgl, pul=pul):
                                for sname, cval in gu_sign[ht]:
                                    ctx.wait(e, "tensor", sname, cval)
                                ctx.wait(e, "tensor", "ve", psgu_evac2[slot][0])
                                ctx.wait(e, "tensor", "se", psgu_evac2[slot][1])
                                if slot == 1:
                                    ctx.wait(e, "tensor", "ve", psA_evac[0])
                                    ctx.wait(e, "tensor", "ve", psA_evac[1])
                                for dc in range(nD):
                                    lg = wgb[b][:, dc * 128:(dc + 1) * 128]
                                    for tg in range(nTG):
                                        ctx.inc(e.matmul(
                                            pgl[tg], lg,
                                            xqT[:, dc * T + tg * TG: dc * T + (tg + 1) * TG],
                                            start=(dc == 0), stop=(dc == nD - 1)), "te")
                                    lu = wub[b][:, dc * 128:(dc + 1) * 128]
                                    for tg in range(nTG):
                                        ctx.inc(e.matmul(
                                            pul[tg], lu,
                                            xqT[:, dc * T + tg * TG: dc * T + (tg + 1) * TG],
                                            start=(dc == 0), stop=(dc == nD - 1)), "te")
                                gu_mm[ht] = ctx.cnt["te"]

                            ctx.emit(tensor=f_te)

                            if hti == 0 and sp >= 1:
                                down_proj(sp - 1)
                                if cache == "read":
                                    load_wd(sp)

                            sgn = []
                            post = []
                            if ht + 2 < cfg.nH:
                                load_gu(ht + 2, sgn, post)
                            if cache != "read" and hti < spanH:
                                load_wd_piece(sp, hti, sgn, post)

                            def f_se3(e, ht=ht, hti=hti, slot=slot, pgl=pgl,
                                      sgn=tuple(sgn)):
                                ctx.wait(e, "scalar", "te", gu_mm[ht])
                                if ht > 0:
                                    ctx.wait(e, "scalar", "ve", pT_mult[ht - 1])
                                for tg in range(nTG):
                                    ctx.inc(e.activation(
                                        out=gsc[:, tg * TG:(tg + 1) * TG],
                                        in_=pgl[tg], func=AF.Silu), "se")
                                psgu_evac2[slot][1] = ctx.cnt["se"]
                                for fn in sgn:
                                    fn(e)

                            def f_ve3(e, ht=ht, hti=hti, slot=slot, pul=pul):
                                ctx.wait(e, "vector", "te", gu_mm[ht])
                                if ht > 0:
                                    ctx.wait(e, "vector", "ve", pT_mult[ht - 1])
                                for tg in range(nTG):
                                    ctx.inc(e.tensor_scalar(
                                        out=usc[:, tg * TG:(tg + 1) * TG],
                                        in0=pul[tg], scalar1=0.5, scalar2=None,
                                        op0=A.mult), "ve")
                                psgu_evac2[slot][0] = ctx.cnt["ve"]
                                ctx.wait(e, "vector", "se")
                                ctx.wait(e, "vector", "ve")
                                if sp > 0:
                                    ctx.wait(e, "vector", "te", wd_mm[sp - 1])
                                ctx.inc(e.tensor_tensor(out=pT[:, hti * T:(hti + 1) * T],
                                                        in0=gsc[:], in1=usc[:], op=A.mult),
                                        "ve")
                                pT_mult[ht] = ctx.cnt["ve"]

                            ctx.emit(scalar=f_se3, vector=f_ve3)
                            for fn in post:
                                fn()
                            emit_cast_piece()
                    down_proj(nSPAN - 1)

                    def f_out(e):
                        ctx.wait(e, "sync", "ve")
                        for tt in range(nT):
                            ctx.inc(e.dma_start(out=dst_t.ap()[tt * 128:(tt + 1) * 128, :],
                                                in_=acc[:, tt * D:(tt + 1) * D]), "qst", 16)

                    ctx.emit(sync=f_out)
                    ctx.barrier(nc)

        def emit_attn(src_t, dst_t):
            BS = cfg.BS
            OG, nOG = cfg.OG, cfg.nOG
            at = contextlib.ExitStack()
            with at:
                mid = at.enter_context(contextlib.ExitStack())
                kT = sb(mid, "kT", [128, nD * T], BF16, side="right")
                vtok = sb(mid, "vtok", [128, nT * D], BF16, side="right")

                wload_tok = {}
                wpr_guard = [0, 0]
                psP_evac = {i: [0, 0] for i in range(4)}
                wq_pairs = (("qga", "qgb"), ("qxc", "qxd"))

                def w_load(e, w_t, wbufs, kind, og, b, single):
                    ctx.wait(e, "sync", "qg", cast_tok.get(kind, 0))
                    ctx.wait(e, "sync", "te", wpr_guard[b])
                    qs = (wq_pairs[0][b],) if single else wq_pairs[b]
                    rh = 128 // len(qs)
                    for hi, q in enumerate(qs):
                        ctx.wait(e, "sync", q)
                        ctx.inc(e.dma_start(
                            out=wbufs["w"][b][hi * rh:(hi + 1) * rh, :],
                            in_=bass.AP(w_t, (og * 128 + hi * rh) * nD * OG,
                                        [[nD * OG, rh], [1, nD * OG]])), q, 16)

                def w_wait(e, b, single):
                    if single:
                        ctx.wait(e, "tensor", wq_pairs[0][b])
                    else:
                        for q in wq_pairs[b]:
                            ctx.wait(e, "tensor", q)

                def proj_fm(w_t, dest, act, kind, wbufs, psP, single=False):
                    xnbuf = wbufs["xn"]
                    nOI = OG // 128
                    gsz = max(1, 4 // (nOI * cfg.nTG))
                    for og0 in range(0, nOG, gsz):
                        evl = []
                        for og in range(og0, min(og0 + gsz, nOG)):
                            b = og % 2

                            def f_qg(e, og=og, b=b):
                                w_load(e, w_t, wbufs, kind, og, b, single)

                            ctx.emit(sync=f_qg)

                            for oi in range(nOI):
                                for tg in range(cfg.nTG):
                                    pb = ((og * nOI + oi) * cfg.nTG + tg) % 4

                                    def f_te(e, og=og, b=b, oi=oi, tg=tg, pb=pb):
                                        w_wait(e, b, single)
                                        ctx.wait(e, "tensor", "ve", psP_evac[pb][0])
                                        ctx.wait(e, "tensor", "se", psP_evac[pb][1])
                                        for dc in range(nD):
                                            ctx.inc(e.matmul(
                                                psP[pb][:, 0:cfg.TG],
                                                wbufs["w"][b][:, dc * OG + oi * 128:
                                                             dc * OG + (oi + 1) * 128],
                                                xnbuf[:, dc * T + tg * cfg.TG:
                                                      dc * T + (tg + 1) * cfg.TG],
                                                start=(dc == 0),
                                                stop=(dc == nD - 1)), "te")

                                    ctx.emit(tensor=f_te)
                                    evl.append((pb, og * nOI + oi, tg,
                                                ctx.cnt["te"]))
                            wload_tok[(kind, og)] = ctx.cnt["te"]
                            wpr_guard[b] = ctx.cnt["te"]

                        def f_se(e, evl=tuple(evl)):
                            for pb, ot, tg, mmtok in evl:
                                ctx.wait(e, "scalar", "te", mmtok)
                                ctx.inc(e.activation(
                                    out=dest[:, ot * T + tg * cfg.TG:
                                             ot * T + (tg + 1) * cfg.TG],
                                    in_=psP[pb][:, 0:cfg.TG], func=act), "se")
                                psP_evac[pb][1] = ctx.cnt["se"]

                        ctx.emit(scalar=f_se)

                def proj_tm(w_t, dest, kind, wbufs, psP, single=False):
                    xnbuf = wbufs["xn"]
                    for og in range(nOG):
                        b = og % 2

                        def f_qg(e, og=og, b=b):
                            w_load(e, w_t, wbufs, kind, og, b, single)

                        ctx.emit(sync=f_qg)

                        se0 = ctx.cnt["se"]
                        mmtoks = []
                        for tt in range(nT):
                            pb = tt % 4

                            def f_te(e, og=og, b=b, tt=tt, pb=pb, se0=se0):
                                w_wait(e, b, single)
                                ctx.wait(e, "tensor", "ve", psP_evac[pb][0])
                                if tt >= 4:
                                    ctx.wait(e, "tensor", "se", se0 + tt - 3)
                                else:
                                    ctx.wait(e, "tensor", "se", psP_evac[pb][1])
                                for dc in range(nD):
                                    ctx.inc(e.matmul(
                                        psP[pb][:, 0:OG],
                                        xnbuf[:, dc * T + tt * 128: dc * T + (tt + 1) * 128],
                                        wbufs["w"][b][:, dc * OG:(dc + 1) * OG],
                                        start=(dc == 0), stop=(dc == nD - 1)), "te")

                            ctx.emit(tensor=f_te)
                            mmtoks.append(ctx.cnt["te"])

                        def f_se(e, og=og, mmtoks=tuple(mmtoks)):
                            for tt in range(nT):
                                ctx.wait(e, "scalar", "te", mmtoks[tt])
                                ctx.inc(e.activation(
                                    out=dest[:, tt * D + og * OG:
                                             tt * D + (og + 1) * OG],
                                    in_=psP[tt % 4][:, 0:OG], func=AF.Silu), "se")

                        ctx.emit(scalar=f_se)
                        assert ctx.cnt["se"] == se0 + nT
                        for pb in range(min(4, nT)):
                            psP_evac[pb][1] = se0 + max(0, nT - 4) + pb + 1
                        wload_tok[(kind, og)] = ctx.cnt["te"]
                        wpr_guard[b] = ctx.cnt["te"]

                with contextlib.ExitStack() as s0:
                    xnT = sb(s0, "xnT", [128, nD * T], BF16)
                    zscAp = [sb(s0, f"zscA{i}", [128, D], F32) for i in range(4)]
                    xtilep = [sb(s0, f"xtile{i}", [128, D], F32) for i in range(4)]
                    zscBp = xtilep
                    wpr = [sb(s0, f"wpr{i}", [128, nD * OG], BF16) for i in range(2)]
                    psTfp = [psb(s0, f"psTf{i}", [128, 512], F32) for i in range(2)]
                    psP = [psb(s0, f"psP{i}", [128, 512], F32) for i in range(4)]

                    emit_stage_a(src_t, xnT, psTfp, zscAp, zscBp,
                                 lambda tt: xtilep[tt % 4][:],
                                 attn_mode=True, spill=True)
                    wb = {"xn": xnT, "w": wpr}
                    proj_fm(akC_t, kT, AF.Silu, "k", wb, psP, single=True)
                    proj_tm(avC_t, vtok, "v", wb, psP)
                    ctx.barrier(nc)

                with contextlib.ExitStack() as s1:
                    kv = sb(s1, "kv", [128, nD * D], BF16)
                    kts = sb(s1, "kts", [128, 2 * nD * 128], BF16)
                    psTb = [psb(s1, f"psTb{i}", [128, 1024], BF16) for i in range(2)]
                    psS = [psb(s1, f"psS{i}", [128, 512], F32) for i in range(2)]

                    psTb_evac = [0, 0]
                    psS_evac = [0, 0]
                    kv_state = {}
                    for c in range(nC):
                        for ri, r0 in enumerate(range(0, 2 * nD, 8)):
                            rn = min(8, 2 * nD - r0)
                            tb = ri % 2

                            def f_te(e, c=c, r0=r0, rn=rn, tb=tb):
                                ctx.wait(e, "tensor", "ve", psTb_evac[tb])
                                for k in range(rn):
                                    idx = r0 + k
                                    jj, dc = idx % 2, idx // 2
                                    ctx.inc(e.transpose(
                                        psTb[tb][:, k * 128:(k + 1) * 128],
                                        kT[:, dc * T + c * BS + jj * 128:
                                           dc * T + c * BS + (jj + 1) * 128],
                                        identb[:]), "te")

                            ctx.emit(tensor=f_te)
                            trtok = ctx.cnt["te"]

                            def f_ve(e, c=c, r0=r0, rn=rn, tb=tb, trtok=trtok):
                                ctx.wait(e, "vector", "te", trtok)
                                for k in range(rn):
                                    idx = r0 + k
                                    jj, dc = idx % 2, idx // 2
                                    ctx.inc(e.tensor_copy(
                                        out=kts[:, jj * nD * 128 + dc * 128:
                                                jj * nD * 128 + (dc + 1) * 128],
                                        in_=psTb[tb][:, k * 128:(k + 1) * 128]), "ve")
                                psTb_evac[tb] = ctx.cnt["ve"]

                            ctx.emit(vector=f_ve)
                        kts_tok = ctx.cnt["ve"]

                        for dt in range(nD):
                            for es in range(D // 512):
                                pb = (dt * (D // 512) + es) % 2

                                def f_te(e, c=c, dt=dt, es=es, pb=pb, kts_tok=kts_tok):
                                    ctx.wait(e, "tensor", "ve", kts_tok)
                                    if c > 0:
                                        ctx.wait(e, "tensor", "ve",
                                                 kv_state.get((dt, es), 0))
                                    ctx.wait(e, "tensor", "ve", psS_evac[pb])
                                    for jj in range(2):
                                        ctx.inc(e.matmul(
                                            psS[pb][:, 0:512],
                                            kts[:, jj * nD * 128 + dt * 128:
                                                jj * nD * 128 + (dt + 1) * 128],
                                            vtok[:, (2 * c + jj) * D + es * 512:
                                                 (2 * c + jj) * D + (es + 1) * 512],
                                            start=(jj == 0), stop=(jj == 1)), "te")

                                ctx.emit(tensor=f_te)
                                mmtok = ctx.cnt["te"]

                                def f_ve(e, c=c, dt=dt, es=es, pb=pb, mmtok=mmtok):
                                    ctx.wait(e, "vector", "te", mmtok)
                                    dst = kv[:, dt * D + es * 512: dt * D + (es + 1) * 512]
                                    if c == 0:
                                        ctx.inc(e.tensor_copy(out=dst,
                                                              in_=psS[pb][:, 0:512]), "ve")
                                    else:
                                        ctx.inc(e.tensor_tensor(out=dst,
                                                                in0=psS[pb][:, 0:512],
                                                                in1=dst, op=A.add), "ve")
                                    kv_state[(dt, es)] = ctx.cnt["ve"]
                                    psS_evac[pb] = ctx.cnt["ve"]

                                ctx.emit(vector=f_ve)

                    def f_snd(e):
                        ctx.wait(e, "sync", "ve")
                        ctx.inc(e.dma_start(out=cc_in_h[0].ap()[:, :], in_=kv[:, 0:HD]),
                                "qs", 16)
                        ctx.inc(e.dma_start(out=cc_in_h[1].ap()[:, :], in_=kv[:, HD:]),
                                "qs", 16)

                    ctx.emit(sync=f_snd)
                    ctx.barrier(nc)

                big = at.enter_context(contextlib.ExitStack())
                oT = sb(big, "oT", [128, nD * T], BF16)
                big2 = at.enter_context(contextlib.ExitStack())
                gT = sb(big2, "gT", [128, nD * T], BF16)
                qT = sb(big2, "qT", [128, nD * T], BF16)
                with contextlib.ExitStack() as s2:
                    xnT2 = sb(s2, "xnT2", [128, nD * T], BF16)
                    wpr2 = [sb(s2, f"wpr2{i}", [128, nD * OG], BF16) for i in range(2)]
                    sTs4 = sb(s2, "sTs4", [128, 4 * 2 * BS], BF16)
                    psP2 = [psb(s2, f"psP2{i}", [128, 512], F32) for i in range(4)]
                    psS2 = [psb(s2, f"psS2{i}", [128, 512], F32) for i in range(2)]
                    psO2 = [psb(s2, f"psO2{i}", [128, 512], F32) for i in range(2)]

                    def f_cc(e):
                        ctx.wait(e, "gpsimd", "qs")
                        for hh in range(2):
                            ctx.inc(e.collective_compute(
                                "AllGather", A.bypass,
                                replica_groups=[[0, 1, 2, 3], [4, 5, 6, 7]],
                                ins=[cc_in_h[hh].ap().opt()],
                                outs=[cc_out_h[hh].ap().opt()]), "cc")

                    def f_ld(e):
                        ctx.inc(e.dma_start(
                            out=xnT2[:],
                            in_=bass.AP(xnT_t, 0, [[T, 128], [128 * T, nD], [1, T]])),
                            "qs", 16)

                    ctx.emit(gpsimd=f_cc, sync=f_ld)
                    xn2tok = ctx.cnt["qs"]

                    def wait_xn2(e):
                        ctx.wait(e, "tensor", "qs", xn2tok)
                    psP_evac = {i: [0, 0] for i in range(4)}
                    wb2 = {"xn": xnT2, "w": wpr2}
                    _orig_emit = ctx.emit

                    def emit_with_xn(**fns):
                        if "tensor" in fns:
                            f = fns["tensor"]

                            def wrapped(e, f=f):
                                wait_xn2(e)
                                f(e)
                            fns["tensor"] = wrapped
                        _orig_emit(**fns)

                    ctx.emit = emit_with_xn
                    proj_fm(aqC_t, qT, AF.Silu, "q", wb2, psP2)
                    proj_fm(agC_t, gT, AF.Sigmoid, "g", wb2, psP2)
                    ctx.emit = _orig_emit

                    psS2_evac = [0, 0]
                    psO2_evac = [0, 0]
                    qT_tok = ctx.cnt["se"]
                    for c in range(nC):
                        for cp in range(c + 1):
                            for jt in range(2):
                                pb = (cp * 2 + jt) % 2

                                def f_te(e, c=c, cp=cp, jt=jt, pb=pb):
                                    ctx.wait(e, "tensor", "se", qT_tok)
                                    ctx.wait(e, "tensor", "ve", psS2_evac[pb])
                                    for dc in range(nD):
                                        ctx.inc(e.matmul(
                                            psS2[pb][:, 0:BS],
                                            kT[:, dc * T + cp * BS + jt * 128:
                                               dc * T + cp * BS + (jt + 1) * 128],
                                            qT[:, dc * T + c * BS: dc * T + c * BS + BS],
                                            start=(dc == 0), stop=(dc == nD - 1)), "te")

                                ctx.emit(tensor=f_te)
                                mmtok = ctx.cnt["te"]

                                def f_ve(e, c=c, cp=cp, jt=jt, pb=pb, mmtok=mmtok):
                                    ctx.wait(e, "vector", "te", mmtok)
                                    dst = sTs4[:, cp * 2 * BS + jt * BS:
                                               cp * 2 * BS + (jt + 1) * BS]
                                    if cp == c:
                                        ctx.inc(e.tensor_tensor(
                                            out=dst, in0=psS2[pb][:, 0:BS],
                                            in1=maskT[jt][:, 0:BS], op=A.mult), "ve")
                                    else:
                                        ctx.inc(e.tensor_copy(
                                            out=dst, in_=psS2[pb][:, 0:BS]), "ve")
                                    psS2_evac[pb] = ctx.cnt["ve"]

                                ctx.emit(vector=f_ve)
                        sblk_tok = ctx.cnt["ve"]

                        for et in range(nD):
                            ob = et % 2

                            def f_te(e, c=c, et=et, ob=ob, sblk_tok=sblk_tok):
                                ctx.wait(e, "tensor", "ve", sblk_tok)
                                ctx.wait(e, "tensor", "ve", psO2_evac[ob])
                                nmm = 2 * (c + 1)
                                i = 0
                                for cp in range(c + 1):
                                    for jj in range(2):
                                        ctx.inc(e.matmul(
                                            psO2[ob][:, 0:BS],
                                            vtok[:, (2 * cp + jj) * D + et * 128:
                                                 (2 * cp + jj) * D + (et + 1) * 128],
                                            sTs4[:, cp * 2 * BS + jj * BS:
                                                 cp * 2 * BS + (jj + 1) * BS],
                                            start=(i == 0), stop=(i == nmm - 1)), "te")
                                        i += 1

                            ctx.emit(tensor=f_te)
                            mmtok = ctx.cnt["te"]

                            def f_ve(e, c=c, et=et, ob=ob, mmtok=mmtok):
                                ctx.wait(e, "vector", "te", mmtok)
                                ctx.inc(e.tensor_tensor(
                                    out=oT[:, et * T + c * BS: et * T + c * BS + BS],
                                    in0=psO2[ob][:, 0:BS],
                                    in1=gT[:, et * T + c * BS: et * T + c * BS + BS],
                                    op=A.mult), "ve")
                                psO2_evac[ob] = ctx.cnt["ve"]

                            ctx.emit(vector=f_ve)
                    ctx.barrier(nc)

                ctx.flush()
                mid.close()

                with contextlib.ExitStack() as s3:
                    S = sb(s3, "S", [128, nD * D], BF16)
                    stg = [sb(s3, f"stg{i}", [128, 2 * D], BF16) for i in range(2)]
                    tmpg = sb(s3, "tmpg", [128, cfg.TG], F32)
                    psP3 = [psb(s3, f"psP3{i}", [128, 512], F32) for i in range(2)]

                    ld_tok = {}
                    S_half_tok = [0, 0]
                    PW = 2 * D
                    nPC = max(1, HD // PW)
                    PW = HD // nPC
                    uid = 0
                    for hh in range(2):
                        for p in range(3):
                            for pc in range(nPC):
                                b = uid % 2

                                def f_ld(e, hh=hh, p=p, pc=pc, b=b, uid=uid):
                                    ctx.wait(e, "sync", "cc", hh + 1)
                                    ctx.wait(e, "sync", "ve", ld_tok.get(uid - 2, 0))
                                    ctx.inc(e.dma_start(
                                        out=stg[b][:, 0:PW],
                                        in_=cc_out_h[hh].ap()[p * 128:(p + 1) * 128,
                                                              pc * PW:(pc + 1) * PW]),
                                        "qsa" if b == 0 else "qsb", 16)

                                ctx.emit(sync=f_ld)
                                dtok = ctx.cnt["qsa" if b == 0 else "qsb"]

                                def f_ve(e, hh=hh, p=p, pc=pc, b=b, dtok=dtok, uid=uid):
                                    ctx.wait(e, "vector", "qsa" if b == 0 else "qsb", dtok)
                                    if p > 0:
                                        ctx.wait(e, "vector", "ve", ld_tok[uid - nPC])
                                    dst = S[:, hh * HD + pc * PW: hh * HD + (pc + 1) * PW]
                                    if p == 0:
                                        ctx.inc(e.tensor_scalar(
                                            out=dst, in0=stg[b][:, 0:PW],
                                            scalar1=prefw[:, 0:1],
                                            scalar2=None, op0=A.mult), "ve")
                                    else:
                                        ctx.inc(e.scalar_tensor_tensor(
                                            out=dst, in0=stg[b][:, 0:PW],
                                            scalar=prefw[:, p:p + 1],
                                            in1=dst, op0=A.mult, op1=A.add), "ve")
                                    ld_tok[uid] = ctx.cnt["ve"]

                                ctx.emit(vector=f_ve)
                                uid += 1
                        S_half_tok[hh] = ctx.cnt["ve"]
                    S_tok = ctx.cnt["ve"]

                    psP3_evac = [0, 0]
                    for hh in range(2):
                        dlo, dhi = hh * (nD // 2), (hh + 1) * (nD // 2)
                        for et in range(nD):
                            for tg in range(cfg.nTG):
                                pb = ((hh * nD + et) * cfg.nTG + tg) % 2

                                def f_te(e, hh=hh, et=et, tg=tg, pb=pb,
                                         dlo=dlo, dhi=dhi):
                                    ctx.wait(e, "tensor", "ve", psP3_evac[pb])
                                    ctx.wait(e, "tensor", "ve", S_half_tok[hh])
                                    for dc in range(dlo, dhi):
                                        ctx.inc(e.matmul(
                                            psP3[pb][:, 0:cfg.TG],
                                            S[:, dc * D + et * 128:
                                              dc * D + (et + 1) * 128],
                                            qT[:, dc * T + tg * cfg.TG:
                                               dc * T + (tg + 1) * cfg.TG],
                                            start=(dc == dlo),
                                            stop=(dc == dhi - 1)), "te")

                                ctx.emit(tensor=f_te)
                                mmtok = ctx.cnt["te"]

                                def f_ve(e, et=et, tg=tg, pb=pb, mmtok=mmtok):
                                    ctx.wait(e, "vector", "te", mmtok)
                                    ctx.wait(e, "vector", "ve")
                                    sl = slice(et * T + tg * cfg.TG,
                                               et * T + (tg + 1) * cfg.TG)
                                    ctx.inc(e.tensor_tensor(
                                        out=tmpg[:], in0=psP3[pb][:, 0:cfg.TG],
                                        in1=gT[:, sl], op=A.mult), "ve")
                                    ctx.wait(e, "vector", "ve")
                                    ctx.inc(e.tensor_tensor(
                                        out=oT[:, sl], in0=tmpg[:],
                                        in1=oT[:, sl], op=A.add), "ve")
                                    psP3_evac[pb] = ctx.cnt["ve"]

                                ctx.emit(vector=f_ve)
                    ctx.barrier(nc)

                ctx.flush()
                big2.close()

                with contextlib.ExitStack() as s4:
                    wpo = [sb(s4, f"wpo{i}", [128, nD * 512], BF16) for i in range(2)]
                    x1s = [sb(s4, f"x1s{i}", [128, 512], F32) for i in range(2)]
                    x2s = [sb(s4, f"x2s{i}", [128, 512], F32) for i in range(2)]
                    psQ = [psb(s4, f"psQ{i}", [128, 512], F32) for i in range(8)]

                    psQ_evac = {i: 0 for i in range(8)}
                    wpo_tok = {}
                    x2_done = {}
                    for ds in range(cfg.nDO):
                        b = ds % 2

                        def f_qg(e, ds=ds, b=b):
                            ctx.wait(e, "sync", "qg", cast_tok.get("p", 0))
                            if ds >= 2:
                                ctx.wait(e, "sync", "te", wpo_tok[ds - 2])
                            for hi, q in enumerate(wq_pairs[b]):
                                ctx.wait(e, "sync", q)
                                ctx.inc(e.dma_start(
                                    out=wpo[b][hi * 64:(hi + 1) * 64, :],
                                    in_=bass.AP(apC_t,
                                                (ds * 128 + hi * 64) * nD * 512,
                                                [[nD * 512, 64], [1, nD * 512]])),
                                        q, 16)

                        ctx.emit(sync=f_qg)

                        for tt in range(nT):
                            def f_te(e, ds=ds, b=b, tt=tt):
                                for q in wq_pairs[b]:
                                    ctx.wait(e, "tensor", q)
                                ctx.wait(e, "tensor", "ve", psQ_evac[tt])
                                for ec in range(nD):
                                    ctx.inc(e.matmul(
                                        psQ[tt][:, 0:512],
                                        oT[:, ec * T + tt * 128: ec * T + (tt + 1) * 128],
                                        wpo[b][:, ec * 512:(ec + 1) * 512],
                                        start=(ec == 0), stop=(ec == nD - 1)), "te")

                            ctx.emit(tensor=f_te)
                            mmtok = ctx.cnt["te"]

                            sbuf = (ds * nT + tt) % 2
                            qx = "qxa" if sbuf == 0 else "qxb"
                            qy = "qy0" if sbuf == 0 else "qy1"

                            def f_ld(e, ds=ds, tt=tt, sbuf=sbuf, qx=qx):
                                ctx.wait(e, "sync", "ve", x2_done.get((ds * nT + tt) - 2, 0))
                                ctx.inc(e.dma_start(
                                    out=x1s[sbuf][:],
                                    in_=src_t.ap()[tt * 128:(tt + 1) * 128,
                                                   ds * 512:(ds + 1) * 512]), qx, 16)

                            ctx.emit(sync=f_ld)
                            ldtok = ctx.cnt[qx]

                            def f_ve(e, ds=ds, tt=tt, sbuf=sbuf, mmtok=mmtok, ldtok=ldtok,
                                     qx=qx, qy=qy):
                                ctx.wait(e, "vector", "te", mmtok)
                                ctx.wait(e, "vector", qx, ldtok)
                                ctx.wait(e, "vector", qy, ctx.cnt[qy])
                                ctx.inc(e.tensor_tensor(out=x2s[sbuf][:],
                                                        in0=psQ[tt][:, 0:512],
                                                        in1=x1s[sbuf][:], op=A.add), "ve")
                                psQ_evac[tt] = ctx.cnt["ve"]
                                x2_done[ds * nT + tt] = ctx.cnt["ve"]

                            ctx.emit(vector=f_ve)
                            vtk = ctx.cnt["ve"]

                            def f_st(e, ds=ds, tt=tt, sbuf=sbuf, vtk=vtk, qy=qy):
                                ctx.wait(e, "sync", "ve", vtk)
                                ctx.inc(e.dma_start(
                                    out=dst_t.ap()[tt * 128:(tt + 1) * 128,
                                                   ds * 512:(ds + 1) * 512],
                                    in_=x2s[sbuf][:]), qy, 16)

                            ctx.emit(sync=f_st)
                        wpo_tok[ds] = ctx.cnt["te"]
                    ctx.barrier(nc)

        emit_consts()
        if "attn" in cfg.phases:
            CPC = 8
            for kind, srcw, dstw in (("k", wkT_t, akC_t), ("v", wvT_t, avC_t),
                                     ("q", wqT_t, aqC_t), ("g", wgT_t, agC_t),
                                     ("p", wpT_t, apC_t)):
                rows = dstw.shape[0]
                rpp = max(128, rows // CPC)
                for r0 in range(0, rows, rpp):
                    cast_pieces.append((kind, srcw, dstw, r0,
                                        min(rpp, rows - r0)))
        src = x0_t
        both_ffn = "ffn1" in cfg.phases and "ffn2" in cfg.phases
        if "ffn1" in cfg.phases:
            dst1 = x1_t if ("attn" in cfg.phases or "ffn2" in cfg.phases) else out_t
            emit_ffn(src, dst1, cache="write" if both_ffn else None)
            src = dst1
        if "attn" in cfg.phases:
            emit_cast_piece(len(cast_pieces))
            dst2 = x2_t if "ffn2" in cfg.phases else out_t
            emit_attn(src, dst2)
            src = dst2
        if "ffn2" in cfg.phases:
            emit_ffn(src, out_t, cache="read" if both_ffn else None)

        ctx.flush()
        fintok = ctx.tok()

        def f_fin(e):
            ctx.wait_tok(e, "sync", fintok)

        ctx.emit(sync=f_fin)
        ctx.flush()

    return nc


def _pack_cols(WT, G):
    Din, Dout = WT.shape
    nD, nG = Din // 128, Dout // G
    R = WT.reshape(nD, 128, nG, G).transpose(2, 1, 0, 3)
    return np.ascontiguousarray(R).reshape(nG * 128, nD * G)


def _host_inputs(cfg, x, Wq, Wk, Wv, Wg, Wp, W_g, W_u, W_d, n_cores=8):
    tr = lambda w: np.ascontiguousarray(np.asarray(w, dtype=np.float32).T)
    wqT = _pack_cols(tr(Wq), cfg.OG)
    wkT = _pack_cols(tr(Wk), cfg.OG)
    wvT = _pack_cols(tr(Wv), cfg.OG)
    wgT = _pack_cols(tr(Wg), cfg.OG)
    wpT = _pack_cols(tr(Wp), cfg.DOS)
    fgT = _pack_cols(tr(W_g), 128)
    fuT = _pack_cols(tr(W_u), 128)
    fdT = tr(W_d)
    x = np.asarray(x, dtype=np.float32)
    nquart = n_cores // x.shape[0]
    in_maps = []
    for c in range(n_cores):
        b, q = c // nquart, c % nquart
        pw = np.zeros((1, 4), np.float32)
        pw[0, :q] = 1.0
        in_maps.append(dict(
            x0=np.ascontiguousarray(x[b, q * cfg.T:(q + 1) * cfg.T, :]),
            wqT=wqT, wkT=wkT, wvT=wvT, wgT=wgT, wpT=wpT,
            fgT=fgT, fuT=fuT, fdT=fdT, prefw=pw))
    return in_maps


def kernel(L0_attn_x, L0_attn_Wq, L0_attn_Wk, L0_attn_Wv, L0_attn_Wg, L0_attn_Wp,
           L0_ffn_W_g, L0_ffn_W_u, L0_ffn_W_d):
    cfg = Cfg()
    nc = build_nc(cfg)
    in_maps = _host_inputs(cfg, L0_attn_x, L0_attn_Wq, L0_attn_Wk, L0_attn_Wv,
                           L0_attn_Wg, L0_attn_Wp, L0_ffn_W_g, L0_ffn_W_u, L0_ffn_W_d)
    res = run_bass_kernel_spmd(nc, in_maps, core_ids=list(range(8)))
    out = np.empty((2, 4096, 2048), np.float32)
    for c in range(8):
        b, q = c // 4, c % 4
        out[b, q * 1024:(q + 1) * 1024, :] = res.results[c]["out"]
    return out
